# version 05
# baseline (speedup 1.0000x reference)
import numpy as np
import ml_dtypes
import concourse.bass as bass
import concourse.mybir as mybir
from concourse.bass_utils import run_bass_kernel_spmd

F32 = mybir.dt.float32
BF16 = mybir.dt.bfloat16
ALU = mybir.AluOpType
ACT = mybir.ActivationFunctionType
AX = mybir.AxisListType

D = 1024
NT = 8
L = 2048
C1 = 256
NC1 = L // C1
C2 = 256
NC2 = L // C2
C3 = 512
NC3 = L // C3
CM = 512
NCM = L // CM
F = 4096
E = 8
LNEPS = 1e-5
WKVEPS = 1e-8
INVD = 1.0 / D
ACT_SET_LN_EXP = 6


class Ch:
    __slots__ = ("h", "n", "eng", "name")

    def __init__(self, nc, name):
        self.h = nc.alloc_semaphore(name)
        self.n = 0
        self.eng = None
        self.name = name


class Prog:
    def __init__(self):
        self.ops = []

    def op(self, eng, fn, waits=(), incs=(), tag=""):
        ws = []
        for w in waits:
            if isinstance(w, tuple):
                chn, v = w
            else:
                chn, v = w, None
            v = chn.n if v is None else v
            if v > 0:
                ws.append((chn, v))
        incs = list(incs)
        for chn, k in incs:
            if k == 16 and chn.n > 0:
                ws.append((chn, chn.n))
        for chn, k in incs:
            if chn.eng is None:
                chn.eng = eng if k == 1 else "dma"
            elif chn.eng != eng or k != 1:
                chn.eng = "mixed"
            chn.n += k
        self.ops.append((eng, fn, ws, incs, tag))

    def replay(self, engine_name, eng):
        last_w = {}
        for (e, fn, ws, incs, tag) in self.ops:
            if e != engine_name:
                continue
            for chn, v in ws:
                if last_w.get(chn, 0) < v:
                    eng.wait_ge(chn.h, v)
                    last_w[chn] = v
            try:
                inst = fn(eng)
            except Exception:
                print(f"REPLAY FAIL at op tag={tag} eng={engine_name}")
                raise
            for ii, (chn, k) in enumerate(incs):
                assert inst is not None, f"op {tag} returned no instruction"
                if ii == 0:
                    inst.then_inc(chn.h, k)
                else:
                    eng.nop().then_inc(chn.h, k)


def build_nc(dbg=False, phases=9):
    nc = bass.Bass("TRN2", target_bir_lowering=False)

    def din(name, shape):
        return nc.dram_tensor(name, shape, F32, kind="ExternalInput").ap()

    xT = din("xT", [D, L])
    Wk_h = din("Wk", [D, D])
    Wv_h = din("Wv", [D, D])
    Wr_h = din("Wr", [D, D])
    Wo_h = din("Wo", [D, D])
    Wfk_h = din("Wfk", [D, F])
    Wfv_h = din("Wfv", [F, D])
    Wfr_h = din("Wfr", [D, D])
    WexpT_h = din("WexpT", [E, D, D])
    WrouterT_h = din("WrouterT", [D, E])
    ln1_g = din("ln1_g", [D])
    ln1_b = din("ln1_b", [D])
    ln2_g = din("ln2_g", [D])
    ln2_b = din("ln2_b", [D])
    mix_k = din("mix_k", [D])
    mix_v = din("mix_v", [D])
    mix_r = din("mix_r", [D])
    td_h = din("td", [D])
    tf_h = din("tf", [D])
    bexp_h = din("b_experts", [E, D])
    brouter_h = din("b_router", [E])

    outT = nc.dram_tensor("outT", [D, L], F32, kind="ExternalOutput").ap()
    kind_scr = "ExternalOutput" if dbg else "Internal"
    xnew_hbm = nc.dram_tensor("xnew_hbm", [D, L], F32, kind=kind_scr).ap()
    x2_hbm = nc.dram_tensor("x2_hbm", [D, L], F32, kind=kind_scr).ap()
    if dbg:
        dbg_xn = nc.dram_tensor("dbg_xn", [128, NT * (C1 + 1)], BF16,
                                kind="ExternalOutput").ap()
        dbg_wkv = nc.dram_tensor("dbg_wkv", [128, NT * C1], BF16,
                                 kind="ExternalOutput").ap()
        dbg_x3 = nc.dram_tensor("dbg_x3", [128, NT * L], F32,
                                kind="ExternalOutput").ap()
        dbg_rw = nc.dram_tensor("dbg_rw", [128, 128], F32,
                                kind="ExternalOutput").ap()

    MB = 17536
    CAP = 211808
    cur = [0]

    def at(name, shape, dtype, off=None, bump=None):
        if off is None:
            off = cur[0]
            per = int(np.prod(shape[1:])) * (2 if dtype == BF16 else 4)
            cur[0] = (off + per + 31) & ~31
            assert cur[0] <= CAP, (name, cur[0])
        else:
            per = int(np.prod(shape[1:])) * (2 if dtype == BF16 else 4)
            assert off + per <= CAP, (name, off)
        return nc.alloc_sbuf_tensor_at(name, shape, dtype, offset=MB + off)

    mixk_c = at("mixk_c", [128, NT], F32)
    mixv_c = at("mixv_c", [128, NT], F32)
    mixr_c = at("mixr_c", [128, NT], F32)
    td_c = at("td_c", [128, NT], F32)
    tf_c = at("tf_c", [128, NT], F32)
    ew_c = at("ew_c", [128, NT], F32)
    etf_c = at("etf_c", [128, NT], F32)
    tmpcol = at("tmpcol", [128, NT], F32)
    ones_bf = at("ones_bf", [128, 1], BF16)
    onesrow_bf = at("onesrow_bf", [1, 512], BF16)
    onesrow_f32 = at("onesrow_f32", [1, 128], F32)
    gb1 = at("gb1", [2, D], BF16)
    gb2 = at("gb2", [2, D], BF16)
    gbu = at("gbu", [2, 128], BF16)
    RA = [at(f"RA{i}", [1, 512], BF16) for i in range(2)]
    RB = [at(f"RB{i}", [2, 512], BF16) for i in range(2)]
    rowM = at("rowM", [1, 512], F32)
    rowW = at("rowW", [1, 512], F32)
    rowT = at("rowT", [1, 512], F32)
    zrow_bf = at("zrow_bf", [1, 128], BF16)
    ident = at("ident", [128, 128], F32)
    icol = at("icol", [128, 1], F32)
    WrT_sb = at("WrT_sb", [128, NT * E], F32)
    brout_row = at("brout_row", [1, E], F32)
    bexp_sb = at("bexp_sb", [E, D], BF16)
    rt = at("rt", [128, 128], F32)
    epsw_col = at("epsw_col", [128, 1], F32)
    carryA = at("carryA", [128, NT], F32)
    carryB = at("carryB", [128, NT], F32)
    assert cur[0] <= 26016, cur[0]

    cur[0] = 26016
    xn2 = [at(f"xn{i}", [128, NT * (C1 + 1)], BF16) for i in range(2)]
    xm_k = at("xm_k", [128, NT * C1], BF16)
    xm_v = at("xm_v", [128, NT * C1], BF16)
    xm_r = at("xm_r", [128, NT * C1], BF16)
    ek2 = [at(f"ek{i}", [128, C1], F32) for i in range(2)]
    p2 = [at(f"p{i}", [128, C1], F32) for i in range(2)]
    r_c = at("r_c", [128, NT * C1], BF16)
    rtmp2l1 = [at(f"rtl1{i}", [128, C1], F32) for i in range(3)]
    wkv2 = [at(f"wkv{i}", [128, NT * C1], BF16) for i in range(2)]
    Asc = [at(f"Asc{d}", [128, C1], F32) for d in range(NT)]
    Bsc = [at(f"Bsc{d}", [128, C1], F32) for d in range(NT)]
    x_c2 = [at(f"x_c{i}", [128, NT * C1], F32) for i in range(2)]
    xnr = [at(f"xnr{i}", [128, C1], F32) for i in range(2)]
    xb2 = [at(f"xb{i}", [128, C1], BF16) for i in range(2)]
    xq2 = [at(f"xq{i}", [128, C1], BF16) for i in range(2)]
    tmp2 = [at(f"tmp{i}", [128, C1], F32) for i in range(3)]
    etmp2 = [at(f"etmp{i}", [128, C1], F32) for i in range(3)]
    dxb = [at(f"dxb{i}", [128, C1], BF16) for i in range(2)]
    assert cur[0] <= 110048, cur[0]

    Wk_sb = at("Wk_sb", [128, NT * D], BF16, off=110048)
    Wv_sb = at("Wv_sb", [128, NT * D], BF16, off=126432)
    Wr_sb = at("Wr_sb", [128, NT * D], BF16, off=142816)
    Wo_sb = at("Wo_sb", [128, NT * D], BF16, off=159200)
    Wfk_q = [None] * 4
    Wfk_q[2] = at("Wfk_q2", [128, NT * D], BF16, off=175616)
    Wfk_q[3] = at("Wfk_q3", [128, NT * D], BF16, off=192000)

    Wfv_q = [at(f"Wfv_q{q}", [128, NT * D], BF16, off=26016 + q * 16384)
             for q in range(4)]
    Wfr_sb = at("Wfr_sb", [128, NT * D], BF16, off=91584)
    Wfk_q[0] = at("Wfk_q0", [128, NT * D], BF16, off=110016)
    Wfk_q[1] = at("Wfk_q1", [128, NT * D], BF16, off=126400)
    l2o = [142816]

    def at2(name, shape, dtype):
        t = at(name, shape, dtype, off=l2o[0])
        per = int(np.prod(shape[1:])) * (2 if dtype == BF16 else 4)
        l2o[0] = (l2o[0] + per + 31) & ~31
        return t

    yn1 = at2("yn1", [128, NT * (C2 + 1)], BF16)
    ym1 = at2("ym1", [128, NT * C2], BF16)
    rr1 = at2("rr1", [128, NT * C2], BF16)
    kk2 = [at2(f"kk{i}", [128, C2], BF16) for i in range(2)]
    xnin = at2("xnin", [128, NT * C2], F32)
    x2cr = [at2(f"x2cr{i}", [128, C2], F32) for i in range(2)]
    yb2 = [at2(f"yb{i}", [128, C2], BF16) for i in range(2)]
    yq2 = [at2(f"yq{i}", [128, C2], BF16) for i in range(2)]
    ytmp2 = [at2(f"ytmp{i}", [128, C2], F32) for i in range(3)]
    rtmp2 = [at2(f"rtmp2_{i}", [128, C2], F32) for i in range(3)]
    assert l2o[0] <= 175552, l2o[0]

    ACC = at("ACC", [128, NT * L], F32, off=26016)
    X3b = at("X3b", [128, NT * L], BF16, off=91584)
    l3o = [124384]

    def at3(name, shape, dtype):
        t = at(name, shape, dtype, off=l3o[0])
        per = int(np.prod(shape[1:])) * (2 if dtype == BF16 else 4)
        l3o[0] = (l3o[0] + per + 31) & ~31
        return t

    rw_sb = at3("rw_sb", [128, 128], F32)
    rwT_sb = at3("rwT_sb", [E, L], BF16)
    cb2 = [at3(f"cb{i}", [128, C3], BF16) for i in range(2)]
    cq2 = [at3(f"cq{i}", [128, C3], BF16) for i in range(2)]
    ctmp2 = [at3(f"ctmp{i}", [128, C3], F32) for i in range(3)]
    zone = l3o[0]
    Zb = [at(f"Zb{i}", [128, NT * CM], BF16, off=zone + i * 8192)
          for i in range(2)]
    rwrep2 = [at(f"rwrep{i}", [128, CM], BF16, off=zone + 16384 + i * 1024)
              for i in range(2)]
    x2in_oc = at("x2in_oc", [128, NT * C3], F32, off=zone)
    wexp_base = ((zone + 18432 + 31) & ~31)
    WexpR = [at(f"WexpR{i}", [128, NT * D], BF16,
                off=wexp_base + i * 16384) for i in range(3)]
    assert wexp_base + 3 * 16384 <= CAP, wexp_base
    xstg = at("xstg", [128, NT * C3], F32, off=wexp_base + 2 * 16384)

    bk = [nc.alloc_psum_tensor(f"bk{i}", [128, 512], F32) for i in range(8)]

    P = Prog()
    CHS = {}

    def ch(name):
        if name not in CHS:
            CHS[name] = Ch(nc, name)
        return CHS[name]

    bCH = [ch(f"bk{i}") for i in range(8)]

    cSTC = ch("stc")
    cSQ = ch("sqs")
    cCAST = ch("castc")
    cSTMM = ch("stmm")
    cROWS = ch("rows")
    cRSTD = ch("rstd")
    cRB = ch("rbrow")
    cREPL = ch("repl")
    cAP1 = ch("ap1ch")
    cAPPLY = ch("apply")

    def emit_ln_stats(C, src_ap, cbr, cqr, sb1, off1, sb2, off2, src_waits,
                      extra_stats_waits=(), tag=""):
        stc_base = cSTC.n
        sq_base = cSQ.n

        def cast(d):
            j = d % 2
            P.op("scalar",
                 lambda e, d=d, j=j: e.copy(cbr[j][:, 0:C], src_ap(d)),
                 waits=list(src_waits) +
                       [(cSTC, max(0, stc_base + 2 * (d - 1)))],
                 incs=[(cCAST, 1)], tag=f"{tag}cast{d}")

        def sq(d):
            j = d % 2
            P.op("scalar",
                 lambda e, d=d, j=j: e.activation(cqr[j][:, 0:C], src_ap(d),
                                                  ACT.Square),
                 incs=[(cSQ, 1)], tag=f"{tag}sq{d}")

        cast(0)
        cast(1)
        for d in range(NT):
            sq(d)
            if d + 2 < NT:
                cast(d + 2)

            def f1(e, j=d % 2, d=d):
                return e.matmul(bk[sb1][0:1, off1:off1 + C], ones_bf[:],
                                cbr[j][:, 0:C], start=(d == 0),
                                stop=(d == NT - 1))

            def f2(e, j=d % 2, d=d):
                return e.matmul(bk[sb2][0:1, off2:off2 + C], ones_bf[:],
                                cqr[j][:, 0:C], start=(d == 0),
                                stop=(d == NT - 1))
            w1 = [(cSQ, sq_base + d + 1)]
            if d == 0:
                w1 += [(cROWS, cROWS.n), (cAPPLY, cAPPLY.n)]
                w1 += list(extra_stats_waits)
            P.op("tensor", f1, waits=w1, incs=[(cSTC, 1)],
                 tag=f"{tag}s1_{d}")
            P.op("tensor", f2, incs=[(cSTC, 1)], tag=f"{tag}s2_{d}")
        cSTC.eng = "mixed"
        P.op("vector",
             lambda e: e.tensor_scalar(rowM[0:1, 0:C], bk[sb1][0:1,
                                                               off1:off1 + C],
                                       -INVD, None, ALU.mult),
             waits=[(cSTC, stc_base + 2 * NT)], tag=f"{tag}rowM")
        P.op("vector",
             lambda e: e.tensor_scalar(rowW[0:1, 0:C],
                                       bk[sb2][0:1, off2:off2 + C],
                                       INVD, LNEPS, ALU.mult, ALU.add),
             waits=[(cRSTD, cRSTD.n)],
             tag=f"{tag}rowW")
        P.op("vector", lambda e: e.drain(), tag=f"{tag}dr1")
        P.op("vector",
             lambda e: e.tensor_tensor(rowT[0:1, 0:C], rowM[0:1, 0:C],
                                       rowM[0:1, 0:C], ALU.mult),
             tag=f"{tag}msq")
        P.op("vector", lambda e: e.drain(), tag=f"{tag}dr2")
        P.op("vector",
             lambda e: e.tensor_tensor(rowW[0:1, 0:C], rowW[0:1, 0:C],
                                       rowT[0:1, 0:C], ALU.subtract),
             incs=[(cROWS, 1)], tag=f"{tag}var")

    def emit_ln_apply(C, src_ap, gb, gb_mod, dst_ap, tmpr, ra_i, rb_i,
                      repl_banks, src_waits, extra_apply_waits=(),
                      extra_apply_incs=None, repl_waits=(), tag=""):
        P.op("scalar",
             lambda e: e.activation(rowW[0:1, 0:C], rowW[0:1, 0:C], ACT.Ln),
             waits=[cROWS], tag=f"{tag}ln")
        P.op("scalar", lambda e: e.drain(), tag=f"{tag}drs")
        P.op("scalar",
             lambda e: e.activation(RA[ra_i][0:1, 0:C], rowW[0:1, 0:C],
                                    ACT.Exp, scale=-0.5),
             waits=[(cRB, cRB.n), (cREPL, cREPL.n)],
             incs=[(cRSTD, 1)], tag=f"{tag}rstd")
        P.op("vector",
             lambda e: e.tensor_tensor(RB[rb_i][0:1, 0:C], rowM[0:1, 0:C],
                                       RA[ra_i][0:1, 0:C], ALU.mult),
             waits=[cRSTD, (cREPL, cREPL.n)], incs=[(cRB, 1)],
             tag=f"{tag}Brow")

        def banks_of(d):
            if len(repl_banks) == 2 and C <= 256:
                b_ = repl_banks[d % 2]
                return b_, b_, 0, C
            if len(repl_banks) == 2:
                return repl_banks[0], repl_banks[1], 0, 0
            return repl_banks[0], repl_banks[0], 0, C

        def ap2(d):
            rb_ag, rb_c, o_ag, o_c = banks_of(d)
            j = d % 3
            P.op("vector", lambda e: e.drain(), tag=f"{tag}apdr{d}")
            incs = [(cAPPLY, 1)]
            if extra_apply_incs:
                incs = incs + extra_apply_incs(d)
            P.op("vector",
                 lambda e, d=d, j=j, rb_c=rb_c, o_c=o_c: e.tensor_tensor(
                     dst_ap(d), tmpr[j][:, 0:C], bk[rb_c][:, o_c:o_c + C],
                     ALU.add),
                 incs=incs, tag=f"{tag}ap2_{d}")

        depth = 2 if (len(repl_banks) == 2 and C <= 256) else 1
        for d in range(NT):
            if d >= 1:
                ap2(d - 1)
            rb_ag, rb_c, o_ag, o_c = banks_of(d)
            go = (d * 128) % gb_mod

            def f_ag(e, go=go, rb_ag=rb_ag, o_ag=o_ag):
                return e.matmul(bk[rb_ag][:, o_ag:o_ag + C],
                                gb[0:1, go:go + 128], RA[ra_i][0:1, 0:C],
                                start=True, stop=True)

            def f_c(e, go=go, rb_c=rb_c, o_c=o_c):
                return e.matmul(bk[rb_c][:, o_c:o_c + C],
                                gb[0:2, go:go + 128], RB[rb_i][0:2, 0:C],
                                start=True, stop=True)
            wts = [cRSTD, (cROWS, cROWS.n), (cAPPLY, cAPPLY.n),
                   (cAP1, max(0, cAP1.n - (depth - 1)))] + list(repl_waits)
            P.op("tensor", f_ag, waits=wts, incs=[(cREPL, 1)],
                 tag=f"{tag}Ag{d}")
            P.op("tensor", f_c, waits=[cRB, (ch("vec_ld"), 144)],
                 incs=[(cREPL, 1)], tag=f"{tag}C{d}")
            j = d % 3
            P.op("vector",
                 lambda e, d=d, j=j, rb_ag=rb_ag, o_ag=o_ag: e.tensor_tensor(
                     tmpr[j][:, 0:C], src_ap(d), bk[rb_ag][:, o_ag:o_ag + C],
                     ALU.mult),
                 waits=[(cREPL, cREPL.n)] + list(src_waits) +
                       list(extra_apply_waits),
                 incs=[(cAP1, 1)], tag=f"{tag}ap1_{d}")
        ap2(NT - 1)

    cVEC = ch("vec_ld")
    cWK = ch("wk_ld")
    cWV = ch("wv_ld")
    cWR = ch("wr_ld")
    cWO = ch("wo_ld")
    cWFKQ = [ch(f"wfkq{q}") for q in range(4)]
    cWFVQ = [ch(f"wfvq{q}") for q in range(4)]
    cWFR = ch("wfr_ld")
    cXLD = ch("x_ld")
    cGB = ch("gb_ld")
    cCONST = ch("const")
    cIDENT = ch("ident")

    def f_tbl(e):
        return e.add_instruction(mybir.InstLoadActFuncSet(
            name=e.bass.get_next_instruction_name(),
            act_func_set_id=ACT_SET_LN_EXP, ins=[], outs=[]))
    P.op("scalar", f_tbl, tag="acttbl")

    for dst, src in [(mixk_c, mix_k), (mixv_c, mix_v), (mixr_c, mix_r),
                     (td_c, td_h), (tf_c, tf_h)]:
        P.op("sync", lambda e, d=dst, s=src: e.dma_start(
            d[:], bass.AP(s.tensor, 0, [[1, 128], [128, NT]]),
            allow_slow_non_contiguous=True),
            incs=[(cVEC, 16)], tag="vecld")
    P.op("sync", lambda e: e.dma_start(
        WrT_sb[:], bass.AP(WrouterT_h.tensor, 0,
                           [[E, 128], [128 * E, NT], [1, E]])),
        incs=[(cVEC, 16)], tag="wrt")
    P.op("sync", lambda e: e.dma_start(
        brout_row[0:1, :], bass.AP(brouter_h.tensor, 0, [[E, 1], [1, E]])),
        incs=[(cVEC, 16)], tag="brr")
    for dst, g_, b_ in [(gb1, ln1_g, ln1_b), (gb2, ln2_g, ln2_b)]:
        P.op("gpsimd", lambda e, d=dst, s=g_: e.dma_start(
            d[0:1, :], bass.AP(s.tensor, 0, [[D, 1], [1, D]])),
            incs=[(cGB, 16)], tag="gld")
        P.op("gpsimd", lambda e, d=dst, s=b_: e.dma_start(
            d[1:2, :], bass.AP(s.tensor, 0, [[D, 1], [1, D]])),
            incs=[(cGB, 16)], tag="bld")
    P.op("gpsimd", lambda e: e.dma_start(bexp_sb[:], bexp_h),
         incs=[(cGB, 16)], tag="bexp")

    P.op("gpsimd", lambda e: e.memset(ones_bf[:], 1.0), incs=[(cCONST, 1)])
    P.op("gpsimd", lambda e: e.memset(epsw_col[:], WKVEPS), incs=[(cCONST, 1)])
    P.op("gpsimd", lambda e: e.memset(onesrow_bf[0:1, :], 1.0),
         incs=[(cCONST, 1)])
    P.op("gpsimd", lambda e: e.memset(onesrow_f32[0:1, :], 1.0),
         incs=[(cCONST, 1)])
    P.op("gpsimd", lambda e: e.memset(zrow_bf[0:1, :], 0.0),
         incs=[(cCONST, 1)])
    P.op("sync", lambda e: e.dma_start(
        gbu[0:1, :], bass.AP(onesrow_bf, 0, [[512, 1], [1, 128]])),
        waits=[cCONST], incs=[(ch("gbu_ld"), 16)], tag="gbu1")
    P.op("sync", lambda e: e.dma_start(
        gbu[1:2, :], bass.AP(zrow_bf, 0, [[128, 1], [1, 128]])),
        waits=[cCONST], incs=[(ch("gbu_ld"), 16)], tag="gbu0")
    P.op("gpsimd", lambda e: e.memset(
        bass.AP(xn2[0], 0, [[NT * (C1 + 1), 128], [C1 + 1, NT], [1, 1]]),
        0.0), incs=[(cCONST, 1)])
    P.op("gpsimd", lambda e: e.iota(
        bass.AP(ident, 0, [[128, 128], [1, 128]]), [[1, 128]],
        channel_multiplier=0, allow_small_or_imprecise_dtypes=True),
        incs=[(cCONST, 1)], tag="iota")
    P.op("gpsimd", lambda e: e.iota(
        bass.AP(icol, 0, [[1, 128], [1, 1]]), [[1, 1]],
        channel_multiplier=1, allow_small_or_imprecise_dtypes=True),
        incs=[(cCONST, 1)], tag="iotacol")
    P.op("vector", lambda e: e.tensor_single_scalar(
        ident[:], ident[:], icol[:, 0:1], ALU.is_equal),
        waits=[cCONST], incs=[(cIDENT, 1)], tag="identeq")
    for i in range(2):
        P.op("sync", lambda e, i=i: e.dma_start(
            RB[i][1:2, :], bass.AP(onesrow_bf, 0, [[512, 1], [1, 512]])),
            waits=[cCONST], incs=[(cVEC, 16)], tag=f"rbones{i}")
    P.op("scalar", lambda e: e.activation(tmpcol[:], td_c[:], ACT.Exp),
         waits=[(cVEC, 5 * 16), cCONST], tag="ew1")
    P.op("scalar", lambda e: e.drain(), tag="ewdr")
    P.op("scalar", lambda e: e.activation(ew_c[:], tmpcol[:], ACT.Exp,
                                          scale=-1.0),
         incs=[(cCONST, 1)], tag="ew2")
    P.op("scalar", lambda e: e.activation(etf_c[:], tf_c[:], ACT.Exp),
         incs=[(cCONST, 1)], tag="etf")

    for dst, src, chn in [(Wk_sb, Wk_h, cWK), (Wv_sb, Wv_h, cWV),
                          (Wr_sb, Wr_h, cWR), (Wo_sb, Wo_h, cWO)]:
        P.op("gpsimd", lambda e, d=dst, s=src: e.dma_start(
            d[:], bass.AP(s.tensor, 0, [[D, 128], [128 * D, NT], [1, D]])),
            incs=[(chn, 16)], tag="wld")
    for q in (2, 3):
        P.op("gpsimd", lambda e, q=q: e.dma_start(
            Wfk_q[q][:], bass.AP(Wfk_h.tensor, q * D,
                                 [[F, 128], [128 * F, NT], [1, D]])),
            incs=[(cWFKQ[q], 16)], tag=f"wfkq{q}")

    def x_src(hbm, c, C):
        return bass.AP(hbm.tensor, c * C, [[L, 128], [128 * L, NT], [1, C]])

    def x_dst(t, C):
        return bass.AP(t, 0, [[NT * C, 128], [C, NT], [1, C]])

    P.op("sync", lambda e: e.dma_start(x_dst(x_c2[0], C1), x_src(xT, 0, C1)),
         incs=[(cXLD, 16)], tag="xld0")
    P.op("sync", lambda e: e.dma_start(x_dst(x_c2[1], C1), x_src(xT, 1, C1)),
         incs=[(cXLD, 16)], tag="xld1")

    cMIXK = ch("mixk")
    cDX = ch("dxg")
    cXMD = ch("xmd")
    cE1 = ch("e1v")
    cMIXV = ch("mixv")
    cMIXR = ch("mixr")
    cBC1 = ch("bcopy1")
    cKMM = ch("kmm")
    cVMM = ch("vmm")
    cRMM = ch("rmm")
    cEK = ch("ekv")
    cP = ch("pv")
    cRS = ch("rsig")
    cSCA = ch("scana")
    cB0 = ch("bzero")
    cSCB = ch("scanb")
    cCARA = ch("cara")
    cCARB = ch("carb")
    cBR = ch("brecip")
    cWKV = ch("wkvv")
    cRW1 = ch("rwgate")
    cWOMM = ch("womm")
    cE = ch("evac")
    cXNST = ch("xnst")

    def xsl(t, d, C, o=0, w=None):
        w = C if w is None else w
        return t[:, d * C + o: d * C + o + w]

    def xnsl(par, d, o, w):
        return xn2[par][:, d * (C1 + 1) + o: d * (C1 + 1) + o + w]

    apply_after = {}

    def l1_ln_stats(c):
        par = c % 2
        emit_ln_stats(
            C1, lambda d, par=par: xsl(x_c2[par], d, C1), xb2, xq2,
            6, 0, 7, 0, src_waits=[(cXLD, 16 * (c + 1))], tag=f"l1s{c}_")

    def l1_ln_apply(c):
        par = c % 2
        emit_ln_apply(
            C1, lambda d, par=par: xsl(x_c2[par], d, C1), gb1, D,
            lambda d, par=par: xnsl(par, d, 1, C1), tmp2, par, par,
            [6, 7], src_waits=[(cXLD, 16 * (c + 1))],
            extra_apply_waits=[cBC1],
            repl_waits=[(CHS["gb_ld"], 64)],
            tag=f"l1a{c}_")
        apply_after[("l1", c)] = cAPPLY.n

    def l1_mixes(c):
        par = c % 2
        aw = (cAPPLY, apply_after[("l1", c)])
        for d in range(NT):
            P.op("gpsimd",
                 lambda e, d=d, par=par: e.tensor_copy(
                     xnsl(1 - par, d, 0, 1), xnsl(par, d, C1, 1)),
                 waits=[aw], incs=[(cBC1, 1)], tag=f"bc{c}_{d}")
        for d in range(NT):
            j = d % 2
            P.op("gpsimd",
                 lambda e, d=d, j=j, par=par: e.tensor_tensor(
                     dxb[j][:, 0:C1], xnsl(par, d, 1, C1),
                     xnsl(par, d, 0, C1), ALU.subtract),
                 waits=[aw, (cMIXK, max(0, 8 * c + d - 1)),
                        (cMIXV, max(0, 8 * c + d - 1)),
                        (cMIXR, max(0, 8 * c + d - 1))],
                 incs=[(cDX, 1)], tag=f"dx{c}_{d}")
            mw = [aw, (cBC1, 8 * (c + 1)), (cDX, cDX.n)]
            if c == 0:
                mw.append((cCONST, cCONST.n))
            P.op("vector",
                 lambda e, d=d, j=j, par=par: e.scalar_tensor_tensor(
                     xsl(xm_k, d, C1), dxb[j][:, 0:C1], mixk_c[:, d:d + 1],
                     xnsl(par, d, 0, C1), ALU.mult, ALU.add),
                 waits=mw + [cKMM], incs=[(cMIXK, 1)], tag=f"xmk{c}_{d}")
            P.op("vector",
                 lambda e, d=d, j=j, par=par: e.scalar_tensor_tensor(
                     xsl(xm_v, d, C1), dxb[j][:, 0:C1], mixv_c[:, d:d + 1],
                     xnsl(par, d, 0, C1), ALU.mult, ALU.add),
                 waits=[cVMM], incs=[(cMIXV, 1)], tag=f"xmv{c}_{d}")
            P.op("vector",
                 lambda e, d=d, j=j, par=par: e.scalar_tensor_tensor(
                     xsl(xm_r, d, C1), dxb[j][:, 0:C1], mixr_c[:, d:d + 1],
                     xnsl(par, d, 0, C1), ALU.mult, ALU.add),
                 waits=[cRMM], incs=[(cMIXR, 1)],
                 tag=f"xmr{c}_{d}")

    def mm_group(bank_i, off, C, lhs_w, rhs_fn, d, waits, incs, tag):
        def f(e, bank_i=bank_i, off=off, C=C, lhs_w=lhs_w, rhs_fn=rhs_fn,
              d=d):
            inst = None
            for kt in range(NT):
                inst = e.matmul(
                    bk[bank_i][:, off:off + C],
                    lhs_w[:, kt * D + d * 128: kt * D + d * 128 + 128],
                    rhs_fn(kt), start=(kt == 0), stop=(kt == NT - 1))
            return inst
        P.op("tensor", f, waits=waits, incs=incs, tag=tag)

    def l1_chunk(c):
        par = c % 2

        def kv_and_evac(d):
            kb = d % 2
            vb = 2 + d % 2
            mm_group(kb, 0, C1, Wk_sb, lambda kt: xsl(xm_k, kt, C1), d,
                     waits=[(cMIXK, 8 * (c + 1)), (cWK, 16),
                            (cEK, max(0, 8 * c + d - 1)), (cE1, 8 * c)],
                     incs=[(cKMM, 1)], tag=f"kmm{c}_{d}")
            mm_group(vb, 0, C1, Wv_sb, lambda kt: xsl(xm_v, kt, C1), d,
                     waits=[(cMIXV, 8 * (c + 1)), (cWV, 16),
                            (cP, max(0, 8 * c + d - 1))],
                     incs=[(cVMM, 1)], tag=f"vmm{c}_{d}")
            P.op("scalar",
                 lambda e, d=d, kb=kb: e.activation(
                     ek2[d % 2][:, 0:C1], bk[kb][:, 0:C1], ACT.Exp),
                 waits=[(cKMM, cKMM.n), (cSCB, max(0, 8 * c + d - 1)),
                        (cP, max(0, 8 * c + d - 1))] +
                       ([(cB0, max(0, d - 1))] if c == 0 else []),
                 incs=[(cEK, 1)], tag=f"ek{c}_{d}")
            P.op("vector",
                 lambda e, d=d, vb=vb: e.tensor_tensor(
                     p2[d % 2][:, 0:C1], bk[vb][:, 0:C1],
                     ek2[d % 2][:, 0:C1], ALU.mult),
                 waits=[(cVMM, cVMM.n), (cEK, cEK.n)],
                 incs=[(cP, 1)], tag=f"p{c}_{d}")

        def scan_d(d):
            P.op("vector", lambda e: e.drain(), tag=f"scdr{c}_{d}")
            if c == 0:
                P.op("vector",
                     lambda e, d=d: e.tensor_tensor(
                         Asc[d][:, 0:1], p2[d % 2][:, 0:1],
                         etf_c[:, d:d + 1], ALU.mult),
                     waits=[(cCONST, cCONST.n)], tag=f"a0_{d}")
                P.op("vector",
                     lambda e, d=d: e.tensor_tensor(
                         Bsc[d][:, 0:1], ek2[d % 2][:, 0:1],
                         etf_c[:, d:d + 1], ALU.mult),
                     incs=[(cB0, 1)], tag=f"b0_{d}")
                P.op("vector", lambda e: e.drain(), tag=f"a0dr{d}")
            W = C1 - 1 if c == 0 else C1
            ew_b = bass.AP(ew_c, d, [[NT, 128], [0, W]])

            def f_sa(e, d=d, c=c, ew_b=ew_b):
                if c == 0:
                    return e.tensor_tensor_scan(
                        Asc[d][:, 1:C1], ew_b, p2[d % 2][:, 1:C1],
                        Asc[d][:, 0:1], ALU.mult, ALU.add)
                return e.tensor_tensor_scan(
                    Asc[d][:, 0:C1], ew_b, p2[d % 2][:, 0:C1],
                    carryA[:, d:d + 1], ALU.mult, ALU.add)

            def f_sb(e, d=d, c=c, ew_b=ew_b):
                if c == 0:
                    return e.tensor_tensor_scan(
                        Bsc[d][:, 1:C1], ew_b, ek2[d % 2][:, 1:C1],
                        Bsc[d][:, 0:1], ALU.mult, ALU.add)
                return e.tensor_tensor_scan(
                    Bsc[d][:, 0:C1], ew_b, ek2[d % 2][:, 0:C1],
                    carryB[:, d:d + 1], ALU.mult, ALU.add)
            sa_w = [(cP, 8 * c + d + 1)]
            sb_w = [(cEK, 8 * c + d + 1)]
            if c > 0:
                sa_w.append((cWKV, 8 * (c - 1) + d + 1))
                sb_w.append((cWKV, 8 * (c - 1) + d + 1))
                sa_w.append((cCARA, 8 * (c - 1) + d + 1))
                sb_w.append((cCARB, 8 * (c - 1) + d + 1))
                sb_w.append((cBR, 8 * (c - 1) + d + 1))
            P.op("vector", f_sa, waits=sa_w, incs=[(cSCA, 1)],
                 tag=f"sca{c}_{d}")
            P.op("vector", f_sb, waits=sb_w, incs=[(cSCB, 1)],
                 tag=f"scb{c}_{d}")
            if c < NC1 - 1:
                P.op("gpsimd",
                     lambda e, d=d: e.tensor_copy(
                         carryA[:, d:d + 1], Asc[d][:, C1 - 1:C1]),
                     waits=[(cSCA, cSCA.n)], incs=[(cCARA, 1)],
                     tag=f"carA{c}_{d}")
                P.op("gpsimd",
                     lambda e, d=d: e.tensor_copy(
                         carryB[:, d:d + 1], Bsc[d][:, C1 - 1:C1]),
                     waits=[(cSCB, cSCB.n)], incs=[(cCARB, 1)],
                     tag=f"carB{c}_{d}")

        def brln(d):
            br_w = [(cSCB, 8 * c + d + 1)]
            if c < NC1 - 1:
                br_w.append((cCARB, 8 * c + d + 1))
            P.op("scalar",
                 lambda e, d=d: e.activation(Bsc[d][:, 0:C1],
                                             Bsc[d][:, 0:C1],
                                             ACT.Ln,
                                             bias=epsw_col[:, 0:1]),
                 waits=br_w, tag=f"brln{c}_{d}")

        def brexp(d):
            P.op("scalar",
                 lambda e, d=d: e.activation(Bsc[d][:, 0:C1],
                                             Bsc[d][:, 0:C1],
                                             ACT.Exp, scale=-1.0),
                 incs=[(cBR, 1)], tag=f"brexp{c}_{d}")

        def wkvm(d):
            wk_w = [(cBR, 8 * c + d + 1), (cSCA, 8 * c + d + 1)]
            if c > 1:
                wk_w.append((cWOMM, 8 * (c - 1)))
            if c < NC1 - 1:
                wk_w.append((cCARA, 8 * c + d + 1))
            P.op("gpsimd",
                 lambda e, d=d, par=par: e.tensor_tensor(
                     xsl(wkv2[par], d, C1), Asc[d][:, 0:C1],
                     Bsc[d][:, 0:C1], ALU.mult),
                 waits=wk_w, incs=[(cWKV, 1)], tag=f"wkv{c}_{d}")

        for d in range(NT + 3):
            if d < NT:
                kv_and_evac(d)
            if 1 <= d < NT + 1:
                scan_d(d - 1)
            if 1 <= d < NT + 1:
                brln(d - 1)
            if 2 <= d < NT + 2:
                P.op("scalar", lambda e: e.drain(), tag=f"brdr{c}_{d}")
                brexp(d - 2)
            if 3 <= d < NT + 3:
                wkvm(d - 3)

        if c + 1 < NC1:
            l1_ln_stats(c + 1)

        def rsig(step):
            P.op("scalar", lambda e: e.drain(), tag=f"rsdr{c}_{step}")
            if step < NT:
                d = step
                rb_ = 4 + d % 2
                mm_group(rb_, 0, C1, Wr_sb, lambda kt: xsl(xm_r, kt, C1), d,
                         waits=[(cMIXR, 8 * (c + 1)), (cWR, 16), bCH[rb_]],
                         incs=[(cRMM, 1)], tag=f"rmm{c}_{d}")
                rs_w = [(cRMM, cRMM.n)]
                if c > 0:
                    rs_w.append((cE, 8 * c))
                P.op("scalar",
                     lambda e, d=d, rb_=rb_: e.activation(
                         rtmp2l1[d % 3][:, 0:C1], bk[rb_][:, 0:C1],
                         ACT.Exp, scale=-1.0),
                     waits=rs_w, incs=[(bCH[rb_], 1)], tag=f"r1_{c}_{d}")
            if step >= 1 and step - 1 < NT:
                d = step - 1
                P.op("scalar",
                     lambda e, d=d: e.activation(
                         rtmp2l1[d % 3][:, 0:C1], rtmp2l1[d % 3][:, 0:C1],
                         ACT.Ln, bias=1.0), tag=f"r2_{c}_{d}")
            if step >= 2:
                d = step - 2
                P.op("scalar",
                     lambda e, d=d: e.activation(
                         xsl(r_c, d, C1), rtmp2l1[d % 3][:, 0:C1],
                         ACT.Exp, scale=-1.0),
                     incs=[(cRS, 1)], tag=f"r3_{c}_{d}")
                P.op("vector",
                     lambda e, d=d, par=par: e.tensor_tensor(
                         xsl(wkv2[par], d, C1), xsl(wkv2[par], d, C1),
                         xsl(r_c, d, C1), ALU.mult),
                     waits=[(cWKV, 8 * c + d + 1), (cRS, 8 * c + d + 1)],
                     incs=[(cRW1, 1)], tag=f"rwg{c}_{d}")

        for step in range(NT + 2):
            rsig(step)

        if c + 1 < NC1:
            l1_ln_apply(c + 1)

        def e2st(d):
            P.op("gpsimd",
                 lambda e, d=d, par=par: e.tensor_tensor(
                     xnr[d % 2][:, 0:C1], etmp2[d % 3][:, 0:C1],
                     xsl(x_c2[par], d, C1), ALU.add),
                 waits=[(cXNST, max(0, 16 * (8 * c + d - 1))),
                        (cE1, 8 * c + d + 1), (cXLD, 16 * (c + 1))],
                 incs=[(cE, 1)], tag=f"e2_{c}_{d}")
            P.op("sync",
                 lambda e, c=c, d=d: e.dma_start(
                     bass.AP(xnew_hbm.tensor, d * 128 * L + c * C1,
                             [[L, 128], [1, C1]]),
                     xnr[d % 2][:, 0:C1]),
                 waits=[(cE, cE.n)], incs=[(cXNST, 16)], tag=f"xnst{c}_{d}")

        for d in range(NT + 1):
            if d < NT:
                ob = d % 2
                mm_group(ob, 0, C1, Wo_sb,
                         lambda kt, par=par: xsl(wkv2[par], kt, C1), d,
                         waits=[(cRW1, 8 * (c + 1)), (cWO, 16),
                                (cEK, 8 * (c + 1)),
                                (cE1, max(0, 8 * c + d - 1))],
                         incs=[(cWOMM, 1)], tag=f"womm{c}_{d}")
                e_w = [(cWOMM, cWOMM.n),
                       (cXNST, max(0, 16 * (8 * c + d - 1)))]
                P.op("vector",
                     lambda e, d=d, ob=ob: e.tensor_copy(
                         etmp2[d % 3][:, 0:C1], bk[ob][:, 0:C1]),
                     waits=e_w + [(cE, max(0, 8 * c + d - 1))],
                     incs=[(cE1, 1)],
                     tag=f"e1_{c}_{d}")
            if d >= 1:
                e2st(d - 1)
        if c + 2 < NC1:
            P.op("sync",
                 lambda e, c=c: e.dma_start(x_dst(x_c2[c % 2], C1),
                                            x_src(xT, c + 2, C1)),
                 waits=[(cE, 8 * (c + 1)), (cCAST, cCAST.n),
                        (cSQ, cSQ.n), (cAP1, cAP1.n)],
                 incs=[(cXLD, 16)],
                 tag=f"xld{c + 2}")

    l1_ln_stats(0)
    l1_ln_apply(0)
    for c in range(NC1):
        l1_mixes(c)
        l1_chunk(c)

    cDBG = ch("dbgs")
    if dbg:
        P.op("sync", lambda e: e.dma_start(dbg_xn, xn2[1][:, :]),
             waits=[(cAPPLY, apply_after[("l1", 1)])], incs=[(cDBG, 16)],
             tag="dbgxn")
        P.op("sync", lambda e: e.dma_start(dbg_wkv, wkv2[0][:, :]),
             waits=[(cWKV, 8)], incs=[(cDBG, 16)], tag="dbgwkv")

    if phases < 2:
        P.op("sync", lambda e: e.nop(),
             waits=[(cXNST, 16 * 8 * NC1)], tag="final_p1")
    if phases >= 2:
        ph1_w = [(chn, chn.n) for chn in
                 [cXNST, cE, cE1, cWOMM, cRS, cBR, cWKV, cSCA, cSCB, cAPPLY,
                  cAP1, cMIXK, cMIXV, cMIXR, cSQ, cSTC, cREPL, cXLD, cDX,
                  cBC1, cCARA, cCARB, cEK, cP, cKMM, cVMM, cRMM]]
        if dbg:
            ph1_w.append((cDBG, 32))
        P.op("gpsimd", lambda e: e.memset(
            bass.AP(yn1, 0, [[NT * (C2 + 1), 128], [C2 + 1, NT], [1, 1]]),
            0.0), waits=ph1_w, incs=[(cCONST, 1)], tag="ynzero")
        for q in (0, 1):
            P.op("gpsimd", lambda e, q=q: e.dma_start(
                Wfk_q[q][:], bass.AP(Wfk_h.tensor, q * D,
                                     [[F, 128], [128 * F, NT], [1, D]])),
                waits=ph1_w, incs=[(cWFKQ[q], 16)], tag=f"wfkq{q}")
        for q in range(4):
            P.op("gpsimd", lambda e, q=q: e.dma_start(
                Wfv_q[q][:], bass.AP(Wfv_h.tensor, q * 8 * 128 * D,
                                     [[D, 128], [128 * D, NT], [1, D]])),
                waits=ph1_w, incs=[(cWFVQ[q], 16)], tag=f"wfvq{q}")
        P.op("gpsimd", lambda e: e.dma_start(
            Wfr_sb[:], bass.AP(Wfr_h.tensor, 0,
                               [[D, 128], [128 * D, NT], [1, D]])),
            waits=ph1_w, incs=[(cWFR, 16)], tag="wfr")

        cXNLD = ch("xnld")
        cYM = ch("ymg")
        cBC2 = ch("bcopy2")
        cRRMM = ch("rrmm")
        cRR = ch("rrs")
        cKKMM = ch("kkmm")
        cRELU = ch("relu")
        cKKSQ = ch("kksq")
        cFFJ = ch("ffj")
        cFFD = ch("ffdone")
        cFF1 = ch("ff1v")
        cX2E = ch("x2evac")
        cX2ST = ch("x2st")

        def ynsl(d, o, w):
            return yn1[:, d * (C2 + 1) + o: d * (C2 + 1) + o + w]

        def l2_ln(c):
            P.op("sync", lambda e, c=c: e.dma_start(
                x_dst(xnin, C2), x_src(xnew_hbm, c, C2)),
                waits=[(cXNST, 16 * 8 * NC1), (cX2E, cX2E.n),
                       (cCAST, cCAST.n), (cSQ, cSQ.n), (cAP1, cAP1.n)],
                incs=[(cXNLD, 16)], tag=f"xnld{c}")
            emit_ln_stats(
                C2, lambda d: xsl(xnin, d, C2), yb2, yq2, 6, 0, 7, 0,
                src_waits=[(cXNLD, 16 * (c + 1))], tag=f"l2s{c}_")
            emit_ln_apply(
                C2, lambda d: xsl(xnin, d, C2), gb2, D,
                lambda d: ynsl(d, 1, C2), ytmp2, c % 2, c % 2,
                [6, 7], src_waits=[(cXNLD, 16 * (c + 1))],
                extra_apply_waits=[cYM, cBC2, cRRMM, cX2E],
                repl_waits=[(CHS["gb_ld"], 64)],
                tag=f"l2a{c}_")
            apply_after[("l2", c)] = cAPPLY.n

        JORD = list(range(16, 32)) + list(range(16))

        def l2_chunk(c):
            aw = (cAPPLY, apply_after[("l2", c)])
            for d in range(NT):
                P.op("gpsimd",
                     lambda e, d=d: e.tensor_tensor(
                         xsl(ym1, d, C2), ynsl(d, 1, C2), ynsl(d, 0, C2),
                         ALU.add),
                     waits=[aw, (cKKMM, cKKMM.n), (cCONST, cCONST.n)],
                     incs=[(cYM, 1)],
                     tag=f"ym{c}_{d}")
            for d in range(NT):
                if c + 1 < NC2:
                    P.op("gpsimd",
                         lambda e, d=d: e.tensor_copy(ynsl(d, 0, 1),
                                                      ynsl(d, C2, 1)),
                         waits=[(cYM, 8 * (c + 1))],
                         incs=[(cBC2, 1)], tag=f"bc2{c}_{d}")
            def rrsig(step):
                P.op("scalar", lambda e: e.drain(), tag=f"rrdr{c}_{step}")
                if step < NT:
                    d = step
                    rb_ = 4 + d % 2
                    mm_group(rb_, 0, C2, Wfr_sb, lambda kt: ynsl(kt, 1, C2), d,
                             waits=[aw, (cWFR, 16), bCH[rb_]],
                             incs=[(cRRMM, 1)], tag=f"rrmm{c}_{d}")
                    rr_w = [(cRRMM, cRRMM.n)]
                    if c > 0:
                        rr_w.append((cX2E, 8 * c))
                    P.op("scalar",
                         lambda e, d=d, rb_=rb_: e.activation(
                             rtmp2[d % 3][:, 0:C2], bk[rb_][:, 0:C2],
                             ACT.Exp, scale=-1.0),
                         waits=rr_w, incs=[(bCH[rb_], 1)], tag=f"rr1_{c}_{d}")
                if step >= 1 and step - 1 < NT:
                    d = step - 1
                    P.op("scalar",
                         lambda e, d=d: e.activation(
                             rtmp2[d % 3][:, 0:C2], rtmp2[d % 3][:, 0:C2],
                             ACT.Ln, bias=1.0), tag=f"rr2_{c}_{d}")
                if step >= 2:
                    d = step - 2
                    P.op("scalar",
                         lambda e, d=d: e.activation(
                             xsl(rr1, d, C2), rtmp2[d % 3][:, 0:C2],
                             ACT.Exp, scale=-1.0),
                         incs=[(cRR, 1)], tag=f"rr3_{c}_{d}")

            for step in range(NT + 2):
                rrsig(step)
            for jj, j in enumerate(JORD):
                q = j // 8
                kb = 4 + jj % 2
                mm_group(kb, 0, C2, Wfk_q[q], lambda kt: xsl(ym1, kt, C2),
                         j % 8,
                         waits=[(cYM, 8 * (c + 1)), (cWFKQ[q], 16),
                                bCH[kb], (cRELU, max(0, cRELU.n - 1))],
                         incs=[(cKKMM, 1)], tag=f"kkmm{c}_{j}")
                P.op("scalar",
                     lambda e, jj=jj, kb=kb: e.activation(
                         kk2[jj % 2][:, 0:C2], bk[kb][:, 0:C2], ACT.Relu,
                         scale=0.5),
                     waits=[(cKKMM, cKKMM.n), (cFFJ, max(0, cFFJ.n - 1))],
                     incs=[(cRELU, 1)], tag=f"relu{c}_{j}")
                P.op("vector",
                     lambda e, jj=jj: e.tensor_tensor(
                         kk2[jj % 2][:, 0:C2], kk2[jj % 2][:, 0:C2],
                         kk2[jj % 2][:, 0:C2], ALU.mult),
                     waits=[(cRELU, cRELU.n)], incs=[(cKKSQ, 1)],
                     tag=f"kksq{c}_{j}")

                ff_w = [(cKKSQ, cKKSQ.n), (cWFVQ[q], 16)]
                if jj == 0:
                    ff_w.append((cFF1, 8 * c))
                if jj < 31:
                    def f_ff(e, jj=jj, j=j, q=q):
                        inst = None
                        for i in range(NT):
                            bt = bk[i // 2]
                            off = (i % 2) * C2
                            inst = e.matmul(
                                bt[:, off:off + C2],
                                Wfv_q[q][:, (j % 8) * D + i * 128:
                                         (j % 8) * D + i * 128 + 128],
                                kk2[jj % 2][:, 0:C2],
                                start=(jj == 0 and i % 2 == 0), stop=False)
                        return inst
                    P.op("tensor", f_ff, waits=ff_w, incs=[(cFFJ, 1)],
                         tag=f"ff{c}_{j}")
                else:
                    for i in range(NT):
                        def f_ffc(e, jj=jj, j=j, q=q, i=i):
                            bt = bk[i // 2]
                            off = (i % 2) * C2
                            return e.matmul(
                                bt[:, off:off + C2],
                                Wfv_q[q][:, (j % 8) * D + i * 128:
                                         (j % 8) * D + i * 128 + 128],
                                kk2[jj % 2][:, 0:C2],
                                start=False, stop=(i % 2 == 1))
                        P.op("tensor", f_ffc, waits=(ff_w if i == 0 else []),
                             incs=([(cFFJ, 1)] if i == NT - 1 else []),
                             tag=f"ffc{c}_{i}")
            def ffe2(i):
                P.op("gpsimd",
                     lambda e, i=i: e.tensor_tensor(
                         x2cr[i % 2][:, 0:C2], ytmp2[i % 3][:, 0:C2],
                         xsl(xnin, i, C2), ALU.add),
                     waits=[(cX2ST, max(0, 16 * (8 * c + i - 1))),
                            (cFF1, 8 * c + i + 1), (cXNLD, 16 * (c + 1))],
                     incs=[(cX2E, 1)], tag=f"ffe2_{c}_{i}")
                P.op("sync",
                     lambda e, c=c, i=i: e.dma_start(
                         bass.AP(x2_hbm.tensor, i * 128 * L + c * C2,
                                 [[L, 128], [1, C2]]),
                         x2cr[i % 2][:, 0:C2]),
                     waits=[(cX2E, cX2E.n)], incs=[(cX2ST, 16)],
                     tag=f"x2st{c}_{i}")

            for i in range(NT + 1):
                if i < NT:
                    bt = bk[i // 2]
                    off = (i % 2) * C2
                    P.op("vector",
                         lambda e, i=i, bt=bt, off=off: e.tensor_tensor(
                             ytmp2[i % 3][:, 0:C2], bt[:, off:off + C2],
                             xsl(rr1, i, C2), ALU.mult),
                         waits=[(cFFJ, 32 * (c + 1)), (cRR, 8 * (c + 1)),
                                (cX2E, max(0, 8 * c + i - 1))],
                         incs=[(cFF1, 1)],
                         tag=f"ffe1_{c}_{i}")
                if i >= 1:
                    ffe2(i - 1)
            if c + 1 < NC2:
                l2_ln(c + 1)

    if phases >= 2:
        l2_ln(0)
        for c in range(NC2):
            l2_chunk(c)
    if phases < 2:
        pass

    if phases < 3:
        P.op("sync", lambda e: e.nop(),
             waits=[(cX2ST, 16 * 8 * NC2)] if phases == 2 else [],
             tag="final_p2")
    if phases >= 4:
        ph2_w = [(chn, chn.n) for chn in
                 [cX2ST, cX2E, cFF1, cFFD, cFFJ, cRR, cAPPLY, cAP1, cYM,
                  cBC2, cSQ, cSTC, cREPL, cXNLD, cRELU, cKKSQ, cKKMM, cRRMM]]
        cEXPW = [ch(f"expw{i}") for i in range(E)]
        for ei in range(2):
            P.op("gpsimd", lambda e, ei=ei: e.dma_start(
                WexpR[ei][:], bass.AP(WexpT_h.tensor, ei * D * D,
                                      [[D, 128], [128 * D, NT], [1, D]])),
                waits=ph2_w, incs=[(cEXPW[ei], 16)], tag=f"expw{ei}")

        cX2LD = ch("x2ld")
        cXST3 = ch("xst3")
        cXAD3 = ch("xad3")
        cX3B = ch("x3b")
        cRTMM = ch("rtmm")
        cRTCP = ch("rtcp")
        cNM = ch("negmax")
        cEEXP = ch("eexp")
        cRW = ch("rwv")
        cTRS = ch("trsp")
        cRWT = ch("rwt")

        def accsl(d, o, w):
            return ACC[:, d * L + o: d * L + o + w]

        def x3bsl(d, o, w):
            return X3b[:, d * L + o: d * L + o + w]

        for c in range(NC3):
            P.op("sync", lambda e, c=c: e.dma_start(
                bass.AP(x2in_oc, 0, [[NT * C3, 128], [1, NT * C3]]),
                bass.AP(x2_hbm.tensor, c * C3,
                        [[L, 128], [128 * L, NT], [1, C3]])),
                waits=ph2_w + [(cX2ST, 16 * 8 * (2 * c + 2)),
                               (cCAST, cCAST.n), (cSQ, cSQ.n),
                               (cAP1, cAP1.n)],
                incs=[(cX2LD, 16)], tag=f"x2ld{c}")
            P.op("sync", lambda e, c=c: e.dma_start(
                bass.AP(xstg, 0, [[NT * C3, 128], [1, NT * C3]]),
                bass.AP(xT.tensor, c * C3,
                        [[L, 128], [128 * L, NT], [1, C3]])),
                waits=ph2_w + ([(cXAD3, c)] if c else []),
                incs=[(cXST3, 16)], tag=f"xst3_{c}")
            for hh in range(2):
                P.op("gpsimd", lambda e, hh=hh: e.tensor_tensor(
                    x2in_oc[:, hh * 2048: hh * 2048 + 2048],
                    x2in_oc[:, hh * 2048: hh * 2048 + 2048],
                    xstg[:, hh * 2048: hh * 2048 + 2048], ALU.add),
                    waits=[(cX2LD, 16 * (c + 1)), (cXST3, 16 * (c + 1))],
                    incs=([(cXAD3, 1)] if hh else []), tag=f"xadd{c}_{hh}")
            emit_ln_stats(
                C3, lambda d: xsl(x2in_oc, d, C3), cb2, cq2, 6, 0, 5, 0,
                src_waits=[(cXAD3, c + 1)],
                extra_stats_waits=([(cRELU, cRELU.n)] if c == 0 else []),
                tag=f"pl1s{c}_")
            emit_ln_apply(
                C3, lambda d: xsl(x2in_oc, d, C3), gbu, 128,
                lambda d, c=c: accsl(d, c * C3, C3), ctmp2, c % 2, c % 2,
                [6, 7], src_waits=[(cXAD3, c + 1)],
                repl_waits=[(CHS["gbu_ld"], 32)],
                tag=f"pl1a{c}_")
            apply_after[("l3", c)] = cAPPLY.n
            aw = (cAPPLY, cAPPLY.n)
            for d in range(NT):
                P.op("scalar",
                     lambda e, d=d, c=c: e.copy(x3bsl(d, c * C3, C3),
                                                accsl(d, c * C3, C3)),
                     waits=[aw], incs=[(cX3B, 1)], tag=f"x3b{c}_{d}")
            for m in range(C3 // 128) if phases >= 5 else []:
                def f_rt(e, c=c, m=m):
                    inst = None
                    for kt in range(NT):
                        inst = e.matmul(
                            bk[4][:, (4 * c + m) * 8: (4 * c + m) * 8 + 8],
                            accsl(kt, c * C3 + m * 128, 128),
                            WrT_sb[:, kt * E: kt * E + E],
                            start=(kt == 0), stop=False)
                    return e.matmul(
                        bk[4][:, (4 * c + m) * 8: (4 * c + m) * 8 + 8],
                        onesrow_f32[0:1, :], brout_row[0:1, :],
                        start=False, stop=True)
                w_rt = [aw, (cVEC, 7 * 16)]
                if c == 0 and m == 0:
                    w_rt.append((cRELU, cRELU.n))
                P.op("tensor", f_rt, waits=w_rt, incs=[(cRTMM, 1)],
                     tag=f"rtmm{c}_{m}")
            if phases >= 5:
                P.op("vector",
                     lambda e, c=c: e.tensor_copy(
                         rw_sb[:, 32 * c: 32 * c + 32],
                         bk[4][:, 32 * c: 32 * c + 32]),
                     waits=[(cRTMM, cRTMM.n)],
                     incs=[(cRTCP, 1)],
                     tag=f"rtcp{c}")

        P.op("gpsimd", lambda e: e.dma_start(
            WexpR[2][:], bass.AP(WexpT_h.tensor, 2 * D * D,
                                 [[D, 128], [128 * D, NT], [1, D]])),
            waits=[(cXAD3, NC3)], incs=[(cEXPW[2], 16)], tag="expw2")
        if phases == 4:
            P.op("sync", lambda e: e.nop(), waits=[(cX3B, 8 * NC3)],
                 tag="final_p4")
        SOFT = phases >= 5
        def rsl(m):
            return rw_sb[:, 8 * m: 8 * m + 8]

        P.op("vector", lambda e: e.drain(), tag="rtdr") if SOFT else None
        for g in range(4) if SOFT else []:
            ms = list(range(4 * g, 4 * g + 4))
            for m in ms:
                P.op("vector", lambda e, m=m: e.reduce_max(
                    rt[:, m % 4: m % 4 + 1], rsl(m), axis=AX.X, negate=True),
                    waits=[(cRTCP, NC3), (cEEXP, cEEXP.n)],
                    incs=[(cNM, 1)], tag=f"nm{m}")
            P.op("vector", lambda e: e.drain(), tag=f"rdr{g}a")
            for m in ms:
                P.op("scalar", lambda e, m=m: e.activation(
                    rsl(m), rsl(m), ACT.Exp, bias=rt[:, m % 4: m % 4 + 1]),
                    waits=[(cNM, cNM.n)], incs=[(cEEXP, 1)], tag=f"ee{m}")
            for m in ms:
                P.op("vector", lambda e, m=m: e.reduce_sum(
                    rt[:, 4 + m % 4: 5 + m % 4], rsl(m), axis=AX.X),
                    waits=[(cEEXP, cEEXP.n)], tag=f"ss{m}")
            P.op("vector", lambda e: e.drain(), tag=f"rdr{g}b")
            for m in ms:
                P.op("vector", lambda e, m=m: e.reciprocal(
                    rt[:, 8 + m % 4: 9 + m % 4], rt[:, 4 + m % 4: 5 + m % 4]),
                    tag=f"rc{m}")
            P.op("vector", lambda e: e.drain(), tag=f"rdr{g}c")
            for m in ms:
                P.op("vector", lambda e, m=m: e.tensor_scalar_mul(
                    rsl(m), rsl(m), rt[:, 8 + m % 4: 9 + m % 4]),
                    tag=f"pr{m}")
            P.op("vector", lambda e: e.drain(), tag=f"rdr{g}d")
            for m in ms:
                P.op("vector", lambda e, m=m: e.reduce_max(
                    rt[:, 12 + m % 4: 13 + m % 4], rsl(m), axis=AX.X),
                    tag=f"m1_{m}")
            P.op("vector", lambda e: e.drain(), tag=f"rdr{g}e")
            for m in ms:
                P.op("vector", lambda e, m=m: e.tensor_single_scalar(
                    rt[:, 24 + (m % 4) * 8: 32 + (m % 4) * 8], rsl(m),
                    rt[:, 12 + m % 4: 13 + m % 4], ALU.is_ge),
                    tag=f"mk_{m}")
            P.op("vector", lambda e: e.drain(), tag=f"rdr{g}h")
            for m in ms:
                P.op("vector", lambda e, m=m: e.scalar_tensor_tensor(
                    rt[:, 56 + (m % 4) * 8: 64 + (m % 4) * 8],
                    rt[:, 24 + (m % 4) * 8: 32 + (m % 4) * 8], -1e30,
                    rsl(m), ALU.mult, ALU.add),
                    tag=f"pm_{m}")
            P.op("vector", lambda e: e.drain(), tag=f"rdr{g}i")
            for m in ms:
                P.op("vector", lambda e, m=m: e.reduce_max(
                    rt[:, 16 + m % 4: 17 + m % 4],
                    rt[:, 56 + (m % 4) * 8: 64 + (m % 4) * 8], axis=AX.X),
                    tag=f"m2_{m}")
            P.op("vector", lambda e: e.drain(), tag=f"rdr{g}f")
            for m in ms:
                P.op("vector", lambda e, m=m: e.tensor_single_scalar(
                    rt[:, 24 + (m % 4) * 8: 32 + (m % 4) * 8], rsl(m),
                    rt[:, 16 + m % 4: 17 + m % 4], ALU.is_ge),
                    tag=f"kp_{m}")
            P.op("vector", lambda e: e.drain(), tag=f"rdr{g}g")
            for m in ms:
                P.op("vector", lambda e, m=m: e.tensor_tensor(
                    rsl(m), rsl(m),
                    rt[:, 24 + (m % 4) * 8: 32 + (m % 4) * 8], ALU.mult),
                    incs=[(cRW, 1)], tag=f"rw_{m}")

        if phases == 5:
            P.op("sync", lambda e: e.nop(), waits=[(cRW, 16)],
                 tag="final_p5")
        for m in range(16) if SOFT else []:
            P.op("tensor",
                 lambda e, m=m: e.transpose(
                     bk[4 + m % 4][0:8, 0:128],
                     rsl(m), ident[:]),
                 waits=[(cRW, m + 1), (cIDENT, 1), (cRTCP, NC3),
                        (cRWT, max(0, cRWT.n - 3))],
                 incs=[(cTRS, 1)], tag=f"trs{m}")
            P.op("vector",
                 lambda e, m=m: e.tensor_copy(
                     rwT_sb[0:8, 128 * m: 128 * m + 128],
                     bk[4 + m % 4][0:8, 0:128]),
                 waits=[(cTRS, m + 1)], incs=[(cRWT, 1)],
                 tag=f"rwte{m}")

        if dbg and cX3B.n > 0:
            P.op("sync", lambda e: e.dma_start(dbg_x3, ACC[:, :]),
                 waits=[(cX3B, 8 * NC3)], incs=[(cDBG, 16)], tag="dbgx3")
        if dbg and SOFT:
            P.op("sync", lambda e: e.dma_start(dbg_rw, rw_sb[:]),
                 waits=[(cRW, 16)], incs=[(cDBG, 16)], tag="dbgrw")

        if phases == 6:
            P.op("sync", lambda e: e.nop(), waits=[(cRWT, 16)],
                 tag="final_p6pre")
        cRWB = ch("rwb")
        cZ = ch("zv")
        cZF = ch("zfree")
        cPMM = ch("pmm")
        cPEV = ch("pev")

        for p in range(4) if phases >= 6 else []:
            for c3 in range(NCM):
                for ei, ex in enumerate((2 * p, 2 * p + 1)):
                    rwb_w = [(cRWT, 16), (cZ, max(0, cZ.n - 1))]
                    if dbg and p == 0 and c3 == 0:
                        rwb_w.append((cDBG, cDBG.n))
                    P.op("sync",
                         lambda e, ex=ex, ei=ei, c3=c3: e.dma_start(
                             rwrep2[ei][:, 0:CM],
                             bass.AP(rwT_sb, ex * L + c3 * CM,
                                     [[L, 1], [0, 128], [1, CM]])),
                         waits=rwb_w, incs=[(cRWB, 16)],
                         tag=f"rwb{p}_{c3}_{ei}")
                    P.op("vector",
                         lambda e, ei=ei, c3=c3: e.tensor_tensor(
                             bass.AP(Zb[ei], 0,
                                     [[NT * CM, 128], [CM, NT], [1, CM]]),
                             bass.AP(X3b, c3 * CM,
                                     [[NT * L, 128], [L, NT], [1, CM]]),
                             bass.AP(rwrep2[ei], 0,
                                     [[CM, 128], [0, NT], [1, CM]]),
                             ALU.mult),
                         waits=[(cRWB, cRWB.n), (cX3B, 8 * NC3),
                                (cPMM, max(0, NCM * p + c3 - 1))],
                         incs=[(cZ, 1)], tag=f"z{p}_{c3}_{ei}")

                    pm_w = [(cZ, cZ.n), (cEXPW[ex], 16)]
                    if ei == 0:
                        if p == 0 and c3 == 0:
                            pm_w.append((cFF1, cFF1.n))
                        elif not (p == 0 and c3 == 0):
                            pm_w.append((cPEV, 8 * (NCM * p + c3)))
                    if ei == 0 or p == 0:
                        def f_pm(e, ex=ex, ei=ei):
                            inst = None
                            wbuf = WexpR[ex % 3]
                            for i in range(NT):
                                for kt in range(NT):
                                    inst = e.matmul(
                                        bk[i][:, 0:CM],
                                        wbuf[:, kt * D + i * 128:
                                             kt * D + i * 128 + 128],
                                        Zb[ei][:, kt * CM: kt * CM + CM],
                                        start=(ei == 0 and kt == 0),
                                        stop=False)
                            return inst
                        P.op("tensor", f_pm, waits=pm_w,
                             tag=f"pmm{p}_{c3}_{ei}")
                    else:
                        for i in range(NT):
                            def f_pmc(e, ex=ex, ei=ei, i=i):
                                inst = None
                                wbuf = WexpR[ex % 3]
                                for kt in range(NT):
                                    inst = e.matmul(
                                        bk[i][:, 0:CM],
                                        wbuf[:, kt * D + i * 128:
                                             kt * D + i * 128 + 128],
                                        Zb[ei][:, kt * CM: kt * CM + CM],
                                        start=False,
                                        stop=(kt == NT - 1))
                                return inst
                            P.op("tensor", f_pmc,
                                 waits=(pm_w if i == 0 else []),
                                 incs=([(cPMM, 1)] if i == NT - 1 else []),
                                 tag=f"pmmc{p}_{c3}_{i}")
                if p == 0:
                    for i in range(NT):
                        def f_bias(e, c3=c3, i=i):
                            return e.matmul(
                                bk[i][:, 0:CM],
                                bexp_sb[0:E, i * 128: i * 128 + 128],
                                bass.AP(rwT_sb, c3 * CM, [[L, E], [1, CM]]),
                                start=False, stop=True)
                        P.op("tensor", f_bias,
                             waits=([(cRWT, 16), (cGB, cGB.n)] if i == 0
                                    else []),
                             incs=([(cPMM, 1)] if i == NT - 1 else []),
                             tag=f"pbias{c3}_{i}")
                for i in range(NT):
                    P.op("vector",
                         lambda e, i=i, c3=c3: e.tensor_tensor(
                             accsl(i, c3 * CM, CM), bk[i][:, 0:CM],
                             accsl(i, c3 * CM, CM), ALU.add),
                         waits=[(cPMM, NCM * p + c3 + 1)],
                         incs=[(cPEV, 1)],
                         tag=f"pev{p}_{c3}_{i}")
            for nx in (2 * p + 3, 2 * p + 4):
                if nx < E:
                    prev = nx - 3
                    pm_val = NCM * (prev // 2 + 1)
                    P.op("gpsimd", lambda e, nx=nx: e.dma_start(
                        WexpR[nx % 3][:],
                        bass.AP(WexpT_h.tensor, nx * D * D,
                                [[D, 128], [128 * D, NT], [1, D]])),
                        waits=[(cPMM, pm_val)],
                        incs=[(cEXPW[nx], 16)], tag=f"expw{nx}")

        if phases == 6:
            P.op("sync", lambda e: e.nop(),
                 waits=[(cPEV, 8 * NCM * 4)], tag="final_p6")
        cOUT = ch("outst")
        for c in range(NC3) if phases >= 9 else []:
            emit_ln_stats(
                C3, lambda d, c=c: accsl(d, c * C3, C3), cb2, cq2, 6, 0, 5, 0,
                src_waits=[(cPEV, cPEV.n)], tag=f"pl2s{c}_")
            emit_ln_apply(
                C3, lambda d, c=c: accsl(d, c * C3, C3), gbu, 128,
                lambda d: xsl(x2in_oc, d, C3), ctmp2, c % 2, c % 2,
                [6, 7], src_waits=[(cPEV, cPEV.n)],
                extra_apply_waits=[(cOUT, 16 * c)],
                repl_waits=[(CHS["gbu_ld"], 32)],
                tag=f"pl2a{c}_")
            P.op("sync",
                 lambda e, c=c: e.dma_start(
                     bass.AP(outT.tensor, c * C3,
                             [[L, 128], [128 * L, NT], [1, C3]]),
                     bass.AP(x2in_oc, 0, [[NT * C3, 128], [1, NT * C3]])),
                 waits=[(cAPPLY, cAPPLY.n)], incs=[(cOUT, 16)],
                 tag=f"out{c}")
        if phases >= 9:
            P.op("sync", lambda e: e.nop(), waits=[(cOUT, 16 * NC3)],
                 tag="final")

    global _LAST_PROG
    _LAST_PROG = P

    with nc.Block() as blk:
        @blk.sync
        def _(eng):
            P.replay("sync", eng)

        @blk.gpsimd
        def _(eng):
            P.replay("gpsimd", eng)

        @blk.tensor
        def _(eng):
            P.replay("tensor", eng)

        @blk.vector
        def _(eng):
            P.replay("vector", eng)

        @blk.scalar
        def _(eng):
            P.replay("scalar", eng)

    return nc


def kernel(**inputs):
    x = np.asarray(inputs["L0_attn_x"])
    n = 8
    common = {
        "Wk": np.asarray(inputs["L0_attn_Wk"]),
        "Wv": np.asarray(inputs["L0_attn_Wv"]),
        "Wr": np.asarray(inputs["L0_attn_Wr"]),
        "Wo": np.asarray(inputs["L0_attn_Wo"]),
        "Wfk": np.asarray(inputs["L0_attn_Wfk"]),
        "Wfv": np.asarray(inputs["L0_attn_Wfv"]),
        "Wfr": np.asarray(inputs["L0_attn_Wfr"]),
        "WexpT": np.ascontiguousarray(
            np.asarray(inputs["L0_ffn_W_experts"]).transpose(0, 2, 1)),
        "WrouterT": np.ascontiguousarray(
            np.asarray(inputs["L0_ffn_W_router"]).T),
        "ln1_g": np.asarray(inputs["L0_attn_ln1_g"]),
        "ln1_b": np.asarray(inputs["L0_attn_ln1_b"]),
        "ln2_g": np.asarray(inputs["L0_attn_ln2_g"]),
        "ln2_b": np.asarray(inputs["L0_attn_ln2_b"]),
        "mix_k": np.asarray(inputs["L0_attn_mix_k"]),
        "mix_v": np.asarray(inputs["L0_attn_mix_v"]),
        "mix_r": np.asarray(inputs["L0_attn_mix_r"]),
        "td": np.asarray(inputs["L0_attn_time_decay"]),
        "tf": np.asarray(inputs["L0_attn_time_first"]),
        "b_experts": np.asarray(inputs["L0_ffn_b_experts"]),
        "b_router": np.asarray(inputs["L0_ffn_b_router"]),
    }
    in_maps = []
    for b in range(n):
        m = dict(common)
        m["xT"] = np.ascontiguousarray(x[b].T)
        in_maps.append(m)
    nc = build_nc()
    res = run_bass_kernel_spmd(nc, in_maps, core_ids=list(range(n)))
    out = np.stack([np.asarray(res.results[b]["outT"]).T for b in range(n)])
    return out.astype(np.float32)


def build_prog_for_check(dbg=False, phases=9):
    build_nc(dbg=dbg, phases=phases)
    return _LAST_PROG, None


# version 06 (final)
# speedup vs baseline: 1.0135x; 1.0135x over previous
import numpy as np
import ml_dtypes
import concourse.bass as bass
import concourse.mybir as mybir
from concourse.bass_utils import run_bass_kernel_spmd

F32 = mybir.dt.float32
BF16 = mybir.dt.bfloat16
ALU = mybir.AluOpType
ACT = mybir.ActivationFunctionType
AX = mybir.AxisListType

D = 1024
NT = 8
L = 2048
C1 = 256
NC1 = L // C1
C2 = 256
NC2 = L // C2
C3 = 512
NC3 = L // C3
CM = 512
NCM = L // CM
F = 4096
E = 8
LNEPS = 1e-5
WKVEPS = 1e-8
INVD = 1.0 / D
ACT_SET_LN_EXP = 6


class Ch:
    __slots__ = ("h", "n", "eng", "name")

    def __init__(self, nc, name):
        self.h = nc.alloc_semaphore(name)
        self.n = 0
        self.eng = None
        self.name = name


class Prog:
    def __init__(self):
        self.ops = []

    def op(self, eng, fn, waits=(), incs=(), tag=""):
        ws = []
        for w in waits:
            if isinstance(w, tuple):
                chn, v = w
            else:
                chn, v = w, None
            v = chn.n if v is None else v
            if v > 0:
                ws.append((chn, v))
        incs = list(incs)
        for chn, k in incs:
            if k == 16 and chn.n > 0:
                ws.append((chn, chn.n))
        for chn, k in incs:
            if chn.eng is None:
                chn.eng = eng if k == 1 else "dma"
            elif chn.eng != eng or k != 1:
                chn.eng = "mixed"
            chn.n += k
        self.ops.append((eng, fn, ws, incs, tag))

    def replay(self, engine_name, eng):
        last_w = {}
        for (e, fn, ws, incs, tag) in self.ops:
            if e != engine_name:
                continue
            for chn, v in ws:
                if last_w.get(chn, 0) < v:
                    eng.wait_ge(chn.h, v)
                    last_w[chn] = v
            try:
                inst = fn(eng)
            except Exception:
                print(f"REPLAY FAIL at op tag={tag} eng={engine_name}")
                raise
            for ii, (chn, k) in enumerate(incs):
                assert inst is not None, f"op {tag} returned no instruction"
                if ii == 0:
                    inst.then_inc(chn.h, k)
                else:
                    eng.nop().then_inc(chn.h, k)


def build_nc(dbg=False, phases=9):
    nc = bass.Bass("TRN2", target_bir_lowering=False)

    def din(name, shape):
        return nc.dram_tensor(name, shape, F32, kind="ExternalInput").ap()

    xT = din("xT", [D, L])
    Wk_h = din("Wk", [D, D])
    Wv_h = din("Wv", [D, D])
    Wr_h = din("Wr", [D, D])
    Wo_h = din("Wo", [D, D])
    Wfk_h = din("Wfk", [D, F])
    Wfv_h = din("Wfv", [F, D])
    Wfr_h = din("Wfr", [D, D])
    WexpT_h = din("WexpT", [E, D, D])
    WrouterT_h = din("WrouterT", [D, E])
    ln1_g = din("ln1_g", [D])
    ln1_b = din("ln1_b", [D])
    ln2_g = din("ln2_g", [D])
    ln2_b = din("ln2_b", [D])
    mix_k = din("mix_k", [D])
    mix_v = din("mix_v", [D])
    mix_r = din("mix_r", [D])
    td_h = din("td", [D])
    tf_h = din("tf", [D])
    bexp_h = din("b_experts", [E, D])
    brouter_h = din("b_router", [E])

    outT = nc.dram_tensor("outT", [D, L], F32, kind="ExternalOutput").ap()
    kind_scr = "ExternalOutput" if dbg else "Internal"
    xnew_hbm = nc.dram_tensor("xnew_hbm", [D, L], F32, kind=kind_scr).ap()
    x2_hbm = nc.dram_tensor("x2_hbm", [D, L], F32, kind=kind_scr).ap()
    if dbg:
        dbg_xn = nc.dram_tensor("dbg_xn", [128, NT * (C1 + 1)], BF16,
                                kind="ExternalOutput").ap()
        dbg_wkv = nc.dram_tensor("dbg_wkv", [128, NT * C1], BF16,
                                 kind="ExternalOutput").ap()
        dbg_x3 = nc.dram_tensor("dbg_x3", [128, NT * L], F32,
                                kind="ExternalOutput").ap()
        dbg_rw = nc.dram_tensor("dbg_rw", [128, 128], F32,
                                kind="ExternalOutput").ap()

    MB = 17536
    CAP = 211808
    cur = [0]

    def at(name, shape, dtype, off=None, bump=None):
        if off is None:
            off = cur[0]
            per = int(np.prod(shape[1:])) * (2 if dtype == BF16 else 4)
            cur[0] = (off + per + 31) & ~31
            assert cur[0] <= CAP, (name, cur[0])
        else:
            per = int(np.prod(shape[1:])) * (2 if dtype == BF16 else 4)
            assert off + per <= CAP, (name, off)
        return nc.alloc_sbuf_tensor_at(name, shape, dtype, offset=MB + off)

    mixk_c = at("mixk_c", [128, NT], F32)
    mixv_c = at("mixv_c", [128, NT], F32)
    mixr_c = at("mixr_c", [128, NT], F32)
    td_c = at("td_c", [128, NT], F32)
    tf_c = at("tf_c", [128, NT], F32)
    ew_c = at("ew_c", [128, NT], F32)
    etf_c = at("etf_c", [128, NT], F32)
    tmpcol = at("tmpcol", [128, NT], F32)
    ones_bf = at("ones_bf", [128, 1], BF16)
    onesrow_bf = at("onesrow_bf", [1, 512], BF16)
    onesrow_f32 = at("onesrow_f32", [1, 128], F32)
    gb1 = at("gb1", [2, D], BF16)
    gb2 = at("gb2", [2, D], BF16)
    gbu = at("gbu", [2, 128], BF16)
    RA = [at(f"RA{i}", [1, 512], BF16) for i in range(2)]
    RB = [at(f"RB{i}", [2, 512], BF16) for i in range(2)]
    rowM = at("rowM", [1, 512], F32)
    rowW = at("rowW", [1, 512], F32)
    rowT = at("rowT", [1, 512], F32)
    zrow_bf = at("zrow_bf", [1, 128], BF16)
    ident = at("ident", [128, 128], F32)
    icol = at("icol", [128, 1], F32)
    WrT_sb = at("WrT_sb", [128, NT * E], F32)
    brout_row = at("brout_row", [1, E], F32)
    bexp_sb = at("bexp_sb", [E, D], BF16)
    rt = at("rt", [128, 128], F32)
    epsw_col = at("epsw_col", [128, 1], F32)
    carryA = at("carryA", [128, NT], F32)
    carryB = at("carryB", [128, NT], F32)
    assert cur[0] <= 26016, cur[0]

    cur[0] = 26016
    xn2 = [at(f"xn{i}", [128, NT * (C1 + 1)], BF16) for i in range(2)]
    xm_k = at("xm_k", [128, NT * C1], BF16)
    xm_v = at("xm_v", [128, NT * C1], BF16)
    xm_r = at("xm_r", [128, NT * C1], BF16)
    ek2 = [at(f"ek{i}", [128, C1], F32) for i in range(2)]
    p2 = [at(f"p{i}", [128, C1], F32) for i in range(2)]
    r_c = at("r_c", [128, NT * C1], BF16)
    rtmp2l1 = [at(f"rtl1{i}", [128, C1], F32) for i in range(3)]
    wkv2 = [at(f"wkv{i}", [128, NT * C1], BF16) for i in range(2)]
    Asc = [at(f"Asc{d}", [128, C1], F32) for d in range(NT)]
    Bsc = [at(f"Bsc{d}", [128, C1], F32) for d in range(NT)]
    x_c2 = [at(f"x_c{i}", [128, NT * C1], F32) for i in range(2)]
    xnr = [at(f"xnr{i}", [128, C1], F32) for i in range(2)]
    xb2 = [at(f"xb{i}", [128, C1], BF16) for i in range(2)]
    xq2 = [at(f"xq{i}", [128, C1], BF16) for i in range(2)]
    tmp2 = [at(f"tmp{i}", [128, C1], F32) for i in range(3)]
    etmp2 = [at(f"etmp{i}", [128, C1], F32) for i in range(3)]
    dxb = [at(f"dxb{i}", [128, C1], BF16) for i in range(2)]
    assert cur[0] <= 110048, cur[0]

    Wk_sb = at("Wk_sb", [128, NT * D], BF16, off=110048)
    Wv_sb = at("Wv_sb", [128, NT * D], BF16, off=126432)
    Wr_sb = at("Wr_sb", [128, NT * D], BF16, off=142816)
    Wo_sb = at("Wo_sb", [128, NT * D], BF16, off=159200)
    Wfk_q = [None] * 4
    Wfk_q[2] = at("Wfk_q2", [128, NT * D], BF16, off=175616)
    Wfk_q[3] = at("Wfk_q3", [128, NT * D], BF16, off=192000)

    Wfv_q = [at(f"Wfv_q{q}", [128, NT * D], BF16, off=26016 + q * 16384)
             for q in range(4)]
    Wfr_sb = at("Wfr_sb", [128, NT * D], BF16, off=91584)
    Wfk_q[0] = at("Wfk_q0", [128, NT * D], BF16, off=110016)
    Wfk_q[1] = at("Wfk_q1", [128, NT * D], BF16, off=126400)
    l2o = [142816]

    def at2(name, shape, dtype):
        t = at(name, shape, dtype, off=l2o[0])
        per = int(np.prod(shape[1:])) * (2 if dtype == BF16 else 4)
        l2o[0] = (l2o[0] + per + 31) & ~31
        return t

    yn1 = at2("yn1", [128, NT * (C2 + 1)], BF16)
    ym1 = at2("ym1", [128, NT * C2], BF16)
    rr1 = at2("rr1", [128, NT * C2], BF16)
    kk2 = [at2(f"kk{i}", [128, C2], BF16) for i in range(2)]
    xnin = at2("xnin", [128, NT * C2], F32)
    x2cr = [at2(f"x2cr{i}", [128, C2], F32) for i in range(2)]
    yb2 = [at2(f"yb{i}", [128, C2], BF16) for i in range(2)]
    yq2 = [at2(f"yq{i}", [128, C2], BF16) for i in range(2)]
    ytmp2 = [at2(f"ytmp{i}", [128, C2], F32) for i in range(3)]
    rtmp2 = [at2(f"rtmp2_{i}", [128, C2], F32) for i in range(3)]
    assert l2o[0] <= 175552, l2o[0]

    ACC = at("ACC", [128, NT * L], F32, off=26016)
    X3b = at("X3b", [128, NT * L], BF16, off=91584)
    l3o = [124384]

    def at3(name, shape, dtype):
        t = at(name, shape, dtype, off=l3o[0])
        per = int(np.prod(shape[1:])) * (2 if dtype == BF16 else 4)
        l3o[0] = (l3o[0] + per + 31) & ~31
        return t

    rw_sb = at3("rw_sb", [128, 128], F32)
    rwT_sb = at3("rwT_sb", [E, L], BF16)
    cb2 = [at3(f"cb{i}", [128, C3], BF16) for i in range(2)]
    cq2 = [at3(f"cq{i}", [128, C3], BF16) for i in range(2)]
    ctmp2 = [at3(f"ctmp{i}", [128, C3], F32) for i in range(3)]
    zone = l3o[0]
    Zb = [at(f"Zb{i}", [128, NT * CM], BF16, off=zone + i * 8192)
          for i in range(2)]
    rwrep2 = [at(f"rwrep{i}", [128, CM], BF16, off=zone + 16384 + i * 1024)
              for i in range(2)]
    x2in_oc = at("x2in_oc", [128, NT * C3], F32, off=zone)
    wexp_base = ((zone + 18432 + 31) & ~31)
    WexpR = [at(f"WexpR{i}", [128, NT * D], BF16,
                off=wexp_base + i * 16384) for i in range(3)]
    assert wexp_base + 3 * 16384 <= CAP, wexp_base
    xstg = at("xstg", [128, NT * C3], F32, off=wexp_base + 2 * 16384)

    bk = [nc.alloc_psum_tensor(f"bk{i}", [128, 512], F32) for i in range(8)]

    P = Prog()
    CHS = {}

    def ch(name):
        if name not in CHS:
            CHS[name] = Ch(nc, name)
        return CHS[name]

    bCH = [ch(f"bk{i}") for i in range(8)]

    cSTC = ch("stc")
    cSQ = ch("sqs")
    cCAST = ch("castc")
    cSTMM = ch("stmm")
    cROWS = ch("rows")
    cRSTD = ch("rstd")
    cRB = ch("rbrow")
    cREPL = ch("repl")
    cAP1 = ch("ap1ch")
    cAPPLY = ch("apply")

    def emit_ln_stats(C, src_ap, cbr, cqr, sb1, off1, sb2, off2, src_waits,
                      extra_stats_waits=(), tag=""):
        stc_base = cSTC.n
        sq_base = cSQ.n

        def cast(d):
            j = d % 2
            P.op("scalar",
                 lambda e, d=d, j=j: e.copy(cbr[j][:, 0:C], src_ap(d)),
                 waits=list(src_waits) +
                       [(cSTC, max(0, stc_base + 2 * (d - 1)))],
                 incs=[(cCAST, 1)], tag=f"{tag}cast{d}")

        def sq(d):
            j = d % 2
            P.op("scalar",
                 lambda e, d=d, j=j: e.activation(cqr[j][:, 0:C], src_ap(d),
                                                  ACT.Square),
                 incs=[(cSQ, 1)], tag=f"{tag}sq{d}")

        cast(0)
        cast(1)
        for d in range(NT):
            sq(d)
            if d + 2 < NT:
                cast(d + 2)

            def f1(e, j=d % 2, d=d):
                return e.matmul(bk[sb1][0:1, off1:off1 + C], ones_bf[:],
                                cbr[j][:, 0:C], start=(d == 0),
                                stop=(d == NT - 1))

            def f2(e, j=d % 2, d=d):
                return e.matmul(bk[sb2][0:1, off2:off2 + C], ones_bf[:],
                                cqr[j][:, 0:C], start=(d == 0),
                                stop=(d == NT - 1))
            w1 = [(cSQ, sq_base + d + 1)]
            if d == 0:
                w1 += [(cROWS, cROWS.n), (cAPPLY, cAPPLY.n)]
                w1 += list(extra_stats_waits)
            P.op("tensor", f1, waits=w1, incs=[(cSTC, 1)],
                 tag=f"{tag}s1_{d}")
            P.op("tensor", f2, incs=[(cSTC, 1)], tag=f"{tag}s2_{d}")
        cSTC.eng = "mixed"
        P.op("vector",
             lambda e: e.tensor_scalar(rowM[0:1, 0:C], bk[sb1][0:1,
                                                               off1:off1 + C],
                                       -INVD, None, ALU.mult),
             waits=[(cSTC, stc_base + 2 * NT)], tag=f"{tag}rowM")
        P.op("vector",
             lambda e: e.tensor_scalar(rowW[0:1, 0:C],
                                       bk[sb2][0:1, off2:off2 + C],
                                       INVD, LNEPS, ALU.mult, ALU.add),
             waits=[(cRSTD, cRSTD.n)],
             tag=f"{tag}rowW")
        P.op("vector", lambda e: e.drain(), tag=f"{tag}dr1")
        P.op("vector",
             lambda e: e.tensor_tensor(rowT[0:1, 0:C], rowM[0:1, 0:C],
                                       rowM[0:1, 0:C], ALU.mult),
             tag=f"{tag}msq")
        P.op("vector", lambda e: e.drain(), tag=f"{tag}dr2")
        P.op("vector",
             lambda e: e.tensor_tensor(rowW[0:1, 0:C], rowW[0:1, 0:C],
                                       rowT[0:1, 0:C], ALU.subtract),
             incs=[(cROWS, 1)], tag=f"{tag}var")

    def emit_ln_apply(C, src_ap, gb, gb_mod, dst_ap, tmpr, ra_i, rb_i,
                      repl_banks, src_waits, extra_apply_waits=(),
                      extra_apply_incs=None, repl_waits=(), tag=""):
        P.op("scalar",
             lambda e: e.activation(rowW[0:1, 0:C], rowW[0:1, 0:C], ACT.Ln),
             waits=[cROWS], tag=f"{tag}ln")
        P.op("scalar", lambda e: e.drain(), tag=f"{tag}drs")
        P.op("scalar",
             lambda e: e.activation(RA[ra_i][0:1, 0:C], rowW[0:1, 0:C],
                                    ACT.Exp, scale=-0.5),
             waits=[(cRB, cRB.n), (cREPL, cREPL.n)],
             incs=[(cRSTD, 1)], tag=f"{tag}rstd")
        P.op("vector",
             lambda e: e.tensor_tensor(RB[rb_i][0:1, 0:C], rowM[0:1, 0:C],
                                       RA[ra_i][0:1, 0:C], ALU.mult),
             waits=[cRSTD, (cREPL, cREPL.n)], incs=[(cRB, 1)],
             tag=f"{tag}Brow")

        def banks_of(d):
            if len(repl_banks) == 2 and C <= 256:
                b_ = repl_banks[d % 2]
                return b_, b_, 0, C
            if len(repl_banks) == 2:
                return repl_banks[0], repl_banks[1], 0, 0
            return repl_banks[0], repl_banks[0], 0, C

        def ap2(d):
            rb_ag, rb_c, o_ag, o_c = banks_of(d)
            j = d % 3
            P.op("vector", lambda e: e.drain(), tag=f"{tag}apdr{d}")
            incs = [(cAPPLY, 1)]
            if extra_apply_incs:
                incs = incs + extra_apply_incs(d)
            P.op("vector",
                 lambda e, d=d, j=j, rb_c=rb_c, o_c=o_c: e.tensor_tensor(
                     dst_ap(d), tmpr[j][:, 0:C], bk[rb_c][:, o_c:o_c + C],
                     ALU.add),
                 incs=incs, tag=f"{tag}ap2_{d}")

        depth = 2 if (len(repl_banks) == 2 and C <= 256) else 1
        for d in range(NT):
            if d >= 1:
                ap2(d - 1)
            rb_ag, rb_c, o_ag, o_c = banks_of(d)
            go = (d * 128) % gb_mod

            def f_ag(e, go=go, rb_ag=rb_ag, o_ag=o_ag):
                return e.matmul(bk[rb_ag][:, o_ag:o_ag + C],
                                gb[0:1, go:go + 128], RA[ra_i][0:1, 0:C],
                                start=True, stop=True)

            def f_c(e, go=go, rb_c=rb_c, o_c=o_c):
                return e.matmul(bk[rb_c][:, o_c:o_c + C],
                                gb[0:2, go:go + 128], RB[rb_i][0:2, 0:C],
                                start=True, stop=True)
            wts = [cRSTD, (cROWS, cROWS.n), (cAPPLY, cAPPLY.n),
                   (cAP1, max(0, cAP1.n - (depth - 1)))] + list(repl_waits)
            P.op("tensor", f_ag, waits=wts, incs=[(cREPL, 1)],
                 tag=f"{tag}Ag{d}")
            P.op("tensor", f_c, waits=[cRB, (ch("vec_ld"), 144)],
                 incs=[(cREPL, 1)], tag=f"{tag}C{d}")
            j = d % 3
            P.op("vector",
                 lambda e, d=d, j=j, rb_ag=rb_ag, o_ag=o_ag: e.tensor_tensor(
                     tmpr[j][:, 0:C], src_ap(d), bk[rb_ag][:, o_ag:o_ag + C],
                     ALU.mult),
                 waits=[(cREPL, cREPL.n)] + list(src_waits) +
                       list(extra_apply_waits),
                 incs=[(cAP1, 1)], tag=f"{tag}ap1_{d}")
        ap2(NT - 1)

    cVEC = ch("vec_ld")
    cWK = ch("wk_ld")
    cWV = ch("wv_ld")
    cWR = ch("wr_ld")
    cWO = ch("wo_ld")
    cWFKQ = [ch(f"wfkq{q}") for q in range(4)]
    cWFVQ = [ch(f"wfvq{q}") for q in range(4)]
    cWFR = ch("wfr_ld")
    cXLD = ch("x_ld")
    cGB = ch("gb_ld")
    cCONST = ch("const")
    cIDENT = ch("ident")

    def f_tbl(e):
        return e.add_instruction(mybir.InstLoadActFuncSet(
            name=e.bass.get_next_instruction_name(),
            act_func_set_id=ACT_SET_LN_EXP, ins=[], outs=[]))
    P.op("scalar", f_tbl, tag="acttbl")

    for dst, src in [(mixk_c, mix_k), (mixv_c, mix_v), (mixr_c, mix_r),
                     (td_c, td_h), (tf_c, tf_h)]:
        P.op("sync", lambda e, d=dst, s=src: e.dma_start(
            d[:], bass.AP(s.tensor, 0, [[1, 128], [128, NT]]),
            allow_slow_non_contiguous=True),
            incs=[(cVEC, 16)], tag="vecld")
    P.op("sync", lambda e: e.dma_start(
        WrT_sb[:], bass.AP(WrouterT_h.tensor, 0,
                           [[E, 128], [128 * E, NT], [1, E]])),
        incs=[(cVEC, 16)], tag="wrt")
    P.op("sync", lambda e: e.dma_start(
        brout_row[0:1, :], bass.AP(brouter_h.tensor, 0, [[E, 1], [1, E]])),
        incs=[(cVEC, 16)], tag="brr")
    for dst, g_, b_ in [(gb1, ln1_g, ln1_b), (gb2, ln2_g, ln2_b)]:
        P.op("gpsimd", lambda e, d=dst, s=g_: e.dma_start(
            d[0:1, :], bass.AP(s.tensor, 0, [[D, 1], [1, D]])),
            incs=[(cGB, 16)], tag="gld")
        P.op("gpsimd", lambda e, d=dst, s=b_: e.dma_start(
            d[1:2, :], bass.AP(s.tensor, 0, [[D, 1], [1, D]])),
            incs=[(cGB, 16)], tag="bld")
    P.op("gpsimd", lambda e: e.dma_start(bexp_sb[:], bexp_h),
         incs=[(cGB, 16)], tag="bexp")

    P.op("gpsimd", lambda e: e.memset(ones_bf[:], 1.0), incs=[(cCONST, 1)])
    P.op("gpsimd", lambda e: e.memset(epsw_col[:], WKVEPS), incs=[(cCONST, 1)])
    P.op("gpsimd", lambda e: e.memset(onesrow_bf[0:1, :], 1.0),
         incs=[(cCONST, 1)])
    P.op("gpsimd", lambda e: e.memset(onesrow_f32[0:1, :], 1.0),
         incs=[(cCONST, 1)])
    P.op("gpsimd", lambda e: e.memset(zrow_bf[0:1, :], 0.0),
         incs=[(cCONST, 1)])
    P.op("sync", lambda e: e.dma_start(
        gbu[0:1, :], bass.AP(onesrow_bf, 0, [[512, 1], [1, 128]])),
        waits=[cCONST], incs=[(ch("gbu_ld"), 16)], tag="gbu1")
    P.op("sync", lambda e: e.dma_start(
        gbu[1:2, :], bass.AP(zrow_bf, 0, [[128, 1], [1, 128]])),
        waits=[cCONST], incs=[(ch("gbu_ld"), 16)], tag="gbu0")
    P.op("gpsimd", lambda e: e.memset(
        bass.AP(xn2[0], 0, [[NT * (C1 + 1), 128], [C1 + 1, NT], [1, 1]]),
        0.0), incs=[(cCONST, 1)])
    P.op("gpsimd", lambda e: e.iota(
        bass.AP(ident, 0, [[128, 128], [1, 128]]), [[1, 128]],
        channel_multiplier=0, allow_small_or_imprecise_dtypes=True),
        incs=[(cCONST, 1)], tag="iota")
    P.op("gpsimd", lambda e: e.iota(
        bass.AP(icol, 0, [[1, 128], [1, 1]]), [[1, 1]],
        channel_multiplier=1, allow_small_or_imprecise_dtypes=True),
        incs=[(cCONST, 1)], tag="iotacol")
    P.op("vector", lambda e: e.tensor_single_scalar(
        ident[:], ident[:], icol[:, 0:1], ALU.is_equal),
        waits=[cCONST], incs=[(cIDENT, 1)], tag="identeq")
    for i in range(2):
        P.op("sync", lambda e, i=i: e.dma_start(
            RB[i][1:2, :], bass.AP(onesrow_bf, 0, [[512, 1], [1, 512]])),
            waits=[cCONST], incs=[(cVEC, 16)], tag=f"rbones{i}")
    P.op("scalar", lambda e: e.activation(tmpcol[:], td_c[:], ACT.Exp),
         waits=[(cVEC, 5 * 16), cCONST], tag="ew1")
    P.op("scalar", lambda e: e.drain(), tag="ewdr")
    P.op("scalar", lambda e: e.activation(ew_c[:], tmpcol[:], ACT.Exp,
                                          scale=-1.0),
         incs=[(cCONST, 1)], tag="ew2")
    P.op("scalar", lambda e: e.activation(etf_c[:], tf_c[:], ACT.Exp),
         incs=[(cCONST, 1)], tag="etf")

    for dst, src, chn in [(Wk_sb, Wk_h, cWK), (Wv_sb, Wv_h, cWV),
                          (Wr_sb, Wr_h, cWR), (Wo_sb, Wo_h, cWO)]:
        P.op("gpsimd", lambda e, d=dst, s=src: e.dma_start(
            d[:], bass.AP(s.tensor, 0, [[D, 128], [128 * D, NT], [1, D]])),
            incs=[(chn, 16)], tag="wld")
    for q in (2, 3):
        P.op("gpsimd", lambda e, q=q: e.dma_start(
            Wfk_q[q][:], bass.AP(Wfk_h.tensor, q * D,
                                 [[F, 128], [128 * F, NT], [1, D]])),
            incs=[(cWFKQ[q], 16)], tag=f"wfkq{q}")

    def x_src(hbm, c, C):
        return bass.AP(hbm.tensor, c * C, [[L, 128], [128 * L, NT], [1, C]])

    def x_dst(t, C):
        return bass.AP(t, 0, [[NT * C, 128], [C, NT], [1, C]])

    P.op("sync", lambda e: e.dma_start(x_dst(x_c2[0], C1), x_src(xT, 0, C1)),
         incs=[(cXLD, 16)], tag="xld0")
    P.op("sync", lambda e: e.dma_start(x_dst(x_c2[1], C1), x_src(xT, 1, C1)),
         incs=[(cXLD, 16)], tag="xld1")

    cMIXK = ch("mixk")
    cDX = ch("dxg")
    cXMD = ch("xmd")
    cE1 = ch("e1v")
    cMIXV = ch("mixv")
    cMIXR = ch("mixr")
    cBC1 = ch("bcopy1")
    cKMM = ch("kmm")
    cVMM = ch("vmm")
    cRMM = ch("rmm")
    cEK = ch("ekv")
    cP = ch("pv")
    cRS = ch("rsig")
    cSCA = ch("scana")
    cB0 = ch("bzero")
    cSCB = ch("scanb")
    cCARA = ch("cara")
    cCARB = ch("carb")
    cBR = ch("brecip")
    cWKV = ch("wkvv")
    cRW1 = ch("rwgate")
    cWOMM = ch("womm")
    cE = ch("evac")
    cXNST = ch("xnst")

    def xsl(t, d, C, o=0, w=None):
        w = C if w is None else w
        return t[:, d * C + o: d * C + o + w]

    def xnsl(par, d, o, w):
        return xn2[par][:, d * (C1 + 1) + o: d * (C1 + 1) + o + w]

    apply_after = {}

    def l1_ln_stats(c):
        par = c % 2
        emit_ln_stats(
            C1, lambda d, par=par: xsl(x_c2[par], d, C1), xb2, xq2,
            6, 0, 7, 0, src_waits=[(cXLD, 16 * (c + 1))], tag=f"l1s{c}_")

    def l1_ln_apply(c):
        par = c % 2
        emit_ln_apply(
            C1, lambda d, par=par: xsl(x_c2[par], d, C1), gb1, D,
            lambda d, par=par: xnsl(par, d, 1, C1), tmp2, par, par,
            [6, 7], src_waits=[(cXLD, 16 * (c + 1))],
            extra_apply_waits=[cBC1],
            repl_waits=[(CHS["gb_ld"], 64)],
            tag=f"l1a{c}_")
        apply_after[("l1", c)] = cAPPLY.n

    def l1_mixes(c):
        par = c % 2
        aw = (cAPPLY, apply_after[("l1", c)])
        for d in range(NT):
            P.op("gpsimd",
                 lambda e, d=d, par=par: e.tensor_copy(
                     xnsl(1 - par, d, 0, 1), xnsl(par, d, C1, 1)),
                 waits=[aw], incs=[(cBC1, 1)], tag=f"bc{c}_{d}")
        for d in range(NT):
            j = d % 2
            P.op("gpsimd",
                 lambda e, d=d, j=j, par=par: e.tensor_tensor(
                     dxb[j][:, 0:C1], xnsl(par, d, 1, C1),
                     xnsl(par, d, 0, C1), ALU.subtract),
                 waits=[aw, (cMIXK, max(0, 8 * c + d - 1)),
                        (cMIXV, max(0, 8 * c + d - 1)),
                        (cMIXR, max(0, 8 * c + d - 1))],
                 incs=[(cDX, 1)], tag=f"dx{c}_{d}")
            mw = [aw, (cBC1, 8 * (c + 1)), (cDX, cDX.n)]
            if c == 0:
                mw.append((cCONST, cCONST.n))
            P.op("vector",
                 lambda e, d=d, j=j, par=par: e.scalar_tensor_tensor(
                     xsl(xm_k, d, C1), dxb[j][:, 0:C1], mixk_c[:, d:d + 1],
                     xnsl(par, d, 0, C1), ALU.mult, ALU.add),
                 waits=mw + [cKMM], incs=[(cMIXK, 1)], tag=f"xmk{c}_{d}")
            P.op("vector",
                 lambda e, d=d, j=j, par=par: e.scalar_tensor_tensor(
                     xsl(xm_v, d, C1), dxb[j][:, 0:C1], mixv_c[:, d:d + 1],
                     xnsl(par, d, 0, C1), ALU.mult, ALU.add),
                 waits=[cVMM], incs=[(cMIXV, 1)], tag=f"xmv{c}_{d}")
            P.op("vector",
                 lambda e, d=d, j=j, par=par: e.scalar_tensor_tensor(
                     xsl(xm_r, d, C1), dxb[j][:, 0:C1], mixr_c[:, d:d + 1],
                     xnsl(par, d, 0, C1), ALU.mult, ALU.add),
                 waits=[cRMM], incs=[(cMIXR, 1)],
                 tag=f"xmr{c}_{d}")

    def mm_group(bank_i, off, C, lhs_w, rhs_fn, d, waits, incs, tag):
        def f(e, bank_i=bank_i, off=off, C=C, lhs_w=lhs_w, rhs_fn=rhs_fn,
              d=d):
            inst = None
            for kt in range(NT):
                inst = e.matmul(
                    bk[bank_i][:, off:off + C],
                    lhs_w[:, kt * D + d * 128: kt * D + d * 128 + 128],
                    rhs_fn(kt), start=(kt == 0), stop=(kt == NT - 1))
            return inst
        P.op("tensor", f, waits=waits, incs=incs, tag=tag)

    def l1_chunk(c):
        par = c % 2

        def kv_and_evac(d):
            kb = d % 2
            vb = 2 + d % 2
            mm_group(kb, 0, C1, Wk_sb, lambda kt: xsl(xm_k, kt, C1), d,
                     waits=[(cMIXK, 8 * (c + 1)), (cWK, 16),
                            (cEK, max(0, 8 * c + d - 1)), (cE1, 8 * c)],
                     incs=[(cKMM, 1)], tag=f"kmm{c}_{d}")
            mm_group(vb, 0, C1, Wv_sb, lambda kt: xsl(xm_v, kt, C1), d,
                     waits=[(cMIXV, 8 * (c + 1)), (cWV, 16),
                            (cP, max(0, 8 * c + d - 1))],
                     incs=[(cVMM, 1)], tag=f"vmm{c}_{d}")
            P.op("scalar",
                 lambda e, d=d, kb=kb: e.activation(
                     ek2[d % 2][:, 0:C1], bk[kb][:, 0:C1], ACT.Exp),
                 waits=[(cKMM, cKMM.n), (cSCB, max(0, 8 * c + d - 1)),
                        (cP, max(0, 8 * c + d - 1))] +
                       ([(cB0, max(0, d - 1))] if c == 0 else []),
                 incs=[(cEK, 1)], tag=f"ek{c}_{d}")
            P.op("vector",
                 lambda e, d=d, vb=vb: e.tensor_tensor(
                     p2[d % 2][:, 0:C1], bk[vb][:, 0:C1],
                     ek2[d % 2][:, 0:C1], ALU.mult),
                 waits=[(cVMM, cVMM.n), (cEK, cEK.n)],
                 incs=[(cP, 1)], tag=f"p{c}_{d}")

        def scan_d(d):
            P.op("vector", lambda e: e.drain(), tag=f"scdr{c}_{d}")
            if c == 0:
                P.op("vector",
                     lambda e, d=d: e.tensor_tensor(
                         Asc[d][:, 0:1], p2[d % 2][:, 0:1],
                         etf_c[:, d:d + 1], ALU.mult),
                     waits=[(cCONST, cCONST.n)], tag=f"a0_{d}")
                P.op("vector",
                     lambda e, d=d: e.tensor_tensor(
                         Bsc[d][:, 0:1], ek2[d % 2][:, 0:1],
                         etf_c[:, d:d + 1], ALU.mult),
                     incs=[(cB0, 1)], tag=f"b0_{d}")
                P.op("vector", lambda e: e.drain(), tag=f"a0dr{d}")
            W = C1 - 1 if c == 0 else C1
            ew_b = bass.AP(ew_c, d, [[NT, 128], [0, W]])

            def f_sa(e, d=d, c=c, ew_b=ew_b):
                if c == 0:
                    return e.tensor_tensor_scan(
                        Asc[d][:, 1:C1], ew_b, p2[d % 2][:, 1:C1],
                        Asc[d][:, 0:1], ALU.mult, ALU.add)
                return e.tensor_tensor_scan(
                    Asc[d][:, 0:C1], ew_b, p2[d % 2][:, 0:C1],
                    carryA[:, d:d + 1], ALU.mult, ALU.add)

            def f_sb(e, d=d, c=c, ew_b=ew_b):
                if c == 0:
                    return e.tensor_tensor_scan(
                        Bsc[d][:, 1:C1], ew_b, ek2[d % 2][:, 1:C1],
                        Bsc[d][:, 0:1], ALU.mult, ALU.add)
                return e.tensor_tensor_scan(
                    Bsc[d][:, 0:C1], ew_b, ek2[d % 2][:, 0:C1],
                    carryB[:, d:d + 1], ALU.mult, ALU.add)
            sa_w = [(cP, 8 * c + d + 1)]
            sb_w = [(cEK, 8 * c + d + 1)]
            if c > 0:
                sa_w.append((cWKV, 8 * (c - 1) + d + 1))
                sb_w.append((cWKV, 8 * (c - 1) + d + 1))
                sa_w.append((cCARA, 8 * (c - 1) + d + 1))
                sb_w.append((cCARB, 8 * (c - 1) + d + 1))
                sb_w.append((cBR, 8 * (c - 1) + d + 1))
            P.op("vector", f_sa, waits=sa_w, incs=[(cSCA, 1)],
                 tag=f"sca{c}_{d}")
            P.op("vector", f_sb, waits=sb_w, incs=[(cSCB, 1)],
                 tag=f"scb{c}_{d}")
            if c < NC1 - 1:
                P.op("gpsimd",
                     lambda e, d=d: e.tensor_copy(
                         carryA[:, d:d + 1], Asc[d][:, C1 - 1:C1]),
                     waits=[(cSCA, cSCA.n)], incs=[(cCARA, 1)],
                     tag=f"carA{c}_{d}")
                P.op("gpsimd",
                     lambda e, d=d: e.tensor_copy(
                         carryB[:, d:d + 1], Bsc[d][:, C1 - 1:C1]),
                     waits=[(cSCB, cSCB.n)], incs=[(cCARB, 1)],
                     tag=f"carB{c}_{d}")

        def brln(d):
            br_w = [(cSCB, 8 * c + d + 1)]
            if c < NC1 - 1:
                br_w.append((cCARB, 8 * c + d + 1))
            P.op("scalar",
                 lambda e, d=d: e.activation(Bsc[d][:, 0:C1],
                                             Bsc[d][:, 0:C1],
                                             ACT.Ln,
                                             bias=epsw_col[:, 0:1]),
                 waits=br_w, tag=f"brln{c}_{d}")

        def brexp(d):
            P.op("scalar",
                 lambda e, d=d: e.activation(Bsc[d][:, 0:C1],
                                             Bsc[d][:, 0:C1],
                                             ACT.Exp, scale=-1.0),
                 incs=[(cBR, 1)], tag=f"brexp{c}_{d}")

        def wkvm(d):
            wk_w = [(cBR, 8 * c + d + 1), (cSCA, 8 * c + d + 1)]
            if c > 1:
                wk_w.append((cWOMM, 8 * (c - 1)))
            if c < NC1 - 1:
                wk_w.append((cCARA, 8 * c + d + 1))
            P.op("gpsimd",
                 lambda e, d=d, par=par: e.tensor_tensor(
                     xsl(wkv2[par], d, C1), Asc[d][:, 0:C1],
                     Bsc[d][:, 0:C1], ALU.mult),
                 waits=wk_w, incs=[(cWKV, 1)], tag=f"wkv{c}_{d}")

        for d in range(NT + 3):
            if d < NT:
                kv_and_evac(d)
            if 1 <= d < NT + 1:
                scan_d(d - 1)
            if 1 <= d < NT + 1:
                brln(d - 1)
            if 2 <= d < NT + 2:
                P.op("scalar", lambda e: e.drain(), tag=f"brdr{c}_{d}")
                brexp(d - 2)
            if 3 <= d < NT + 3:
                wkvm(d - 3)

        if c + 1 < NC1:
            l1_ln_stats(c + 1)

        def rsig(step):
            P.op("scalar", lambda e: e.drain(), tag=f"rsdr{c}_{step}")
            if step < NT:
                d = step
                rb_ = 4 + d % 2
                mm_group(rb_, 0, C1, Wr_sb, lambda kt: xsl(xm_r, kt, C1), d,
                         waits=[(cMIXR, 8 * (c + 1)), (cWR, 16), bCH[rb_]],
                         incs=[(cRMM, 1)], tag=f"rmm{c}_{d}")
                rs_w = [(cRMM, cRMM.n)]
                if c > 0:
                    rs_w.append((cE, 8 * c))
                P.op("scalar",
                     lambda e, d=d, rb_=rb_: e.activation(
                         rtmp2l1[d % 3][:, 0:C1], bk[rb_][:, 0:C1],
                         ACT.Exp, scale=-1.0),
                     waits=rs_w, incs=[(bCH[rb_], 1)], tag=f"r1_{c}_{d}")
            if step >= 1 and step - 1 < NT:
                d = step - 1
                P.op("scalar",
                     lambda e, d=d: e.activation(
                         rtmp2l1[d % 3][:, 0:C1], rtmp2l1[d % 3][:, 0:C1],
                         ACT.Ln, bias=1.0), tag=f"r2_{c}_{d}")
            if step >= 2:
                d = step - 2
                P.op("scalar",
                     lambda e, d=d: e.activation(
                         xsl(r_c, d, C1), rtmp2l1[d % 3][:, 0:C1],
                         ACT.Exp, scale=-1.0),
                     incs=[(cRS, 1)], tag=f"r3_{c}_{d}")
                P.op("vector",
                     lambda e, d=d, par=par: e.tensor_tensor(
                         xsl(wkv2[par], d, C1), xsl(wkv2[par], d, C1),
                         xsl(r_c, d, C1), ALU.mult),
                     waits=[(cWKV, 8 * c + d + 1), (cRS, 8 * c + d + 1)],
                     incs=[(cRW1, 1)], tag=f"rwg{c}_{d}")

        for step in range(NT + 2):
            rsig(step)

        if c + 1 < NC1:
            l1_ln_apply(c + 1)

        def e2st(d):
            P.op("gpsimd",
                 lambda e, d=d, par=par: e.tensor_tensor(
                     xnr[d % 2][:, 0:C1], etmp2[d % 3][:, 0:C1],
                     xsl(x_c2[par], d, C1), ALU.add),
                 waits=[(cXNST, max(0, 16 * (8 * c + d - 1))),
                        (cE1, 8 * c + d + 1), (cXLD, 16 * (c + 1))],
                 incs=[(cE, 1)], tag=f"e2_{c}_{d}")
            P.op("sync",
                 lambda e, c=c, d=d: e.dma_start(
                     bass.AP(xnew_hbm.tensor, d * 128 * L + c * C1,
                             [[L, 128], [1, C1]]),
                     xnr[d % 2][:, 0:C1]),
                 waits=[(cE, cE.n)], incs=[(cXNST, 16)], tag=f"xnst{c}_{d}")

        for d in range(NT + 1):
            if d < NT:
                ob = d % 2
                mm_group(ob, 0, C1, Wo_sb,
                         lambda kt, par=par: xsl(wkv2[par], kt, C1), d,
                         waits=[(cRW1, 8 * (c + 1)), (cWO, 16),
                                (cEK, 8 * (c + 1)),
                                (cE1, max(0, 8 * c + d - 1))],
                         incs=[(cWOMM, 1)], tag=f"womm{c}_{d}")
                e_w = [(cWOMM, cWOMM.n),
                       (cXNST, max(0, 16 * (8 * c + d - 1)))]
                P.op("vector",
                     lambda e, d=d, ob=ob: e.tensor_copy(
                         etmp2[d % 3][:, 0:C1], bk[ob][:, 0:C1]),
                     waits=e_w + [(cE, max(0, 8 * c + d - 1))],
                     incs=[(cE1, 1)],
                     tag=f"e1_{c}_{d}")
            if d >= 1:
                e2st(d - 1)
        if c + 2 < NC1:
            P.op("sync",
                 lambda e, c=c: e.dma_start(x_dst(x_c2[c % 2], C1),
                                            x_src(xT, c + 2, C1)),
                 waits=[(cE, 8 * (c + 1)), (cCAST, cCAST.n),
                        (cSQ, cSQ.n), (cAP1, cAP1.n)],
                 incs=[(cXLD, 16)],
                 tag=f"xld{c + 2}")

    l1_ln_stats(0)
    l1_ln_apply(0)
    for c in range(NC1):
        l1_mixes(c)
        if c == NC1 - 1:
            pre01 = [(cKMM, 8 * NC1), (cVMM, 8 * NC1), (cMIXK, 8 * NC1),
                     (cMIXV, 8 * NC1), (cMIXR, 8 * NC1), (cDX, 8 * NC1)]
            for q in (0, 1):
                P.op("gpsimd", lambda e, q=q: e.dma_start(
                    Wfk_q[q][:], bass.AP(Wfk_h.tensor, q * D,
                                         [[F, 128], [128 * F, NT], [1, D]])),
                    waits=pre01, incs=[(cWFKQ[q], 16)], tag=f"wfkq{q}")
        l1_chunk(c)

    cDBG = ch("dbgs")
    if dbg:
        P.op("sync", lambda e: e.dma_start(dbg_xn, xn2[1][:, :]),
             waits=[(cAPPLY, apply_after[("l1", 1)])], incs=[(cDBG, 16)],
             tag="dbgxn")
        P.op("sync", lambda e: e.dma_start(dbg_wkv, wkv2[0][:, :]),
             waits=[(cWKV, 8)], incs=[(cDBG, 16)], tag="dbgwkv")

    if phases < 2:
        P.op("sync", lambda e: e.nop(),
             waits=[(cXNST, 16 * 8 * NC1)], tag="final_p1")
    if phases >= 2:
        ph1_w = [(chn, chn.n) for chn in
                 [cXNST, cE, cE1, cWOMM, cRS, cBR, cWKV, cSCA, cSCB, cAPPLY,
                  cAP1, cMIXK, cMIXV, cMIXR, cSQ, cSTC, cREPL, cXLD, cDX,
                  cBC1, cCARA, cCARB, cEK, cP, cKMM, cVMM, cRMM]]
        if dbg:
            ph1_w.append((cDBG, 32))
        P.op("gpsimd", lambda e: e.memset(
            bass.AP(yn1, 0, [[NT * (C2 + 1), 128], [C2 + 1, NT], [1, 1]]),
            0.0), waits=ph1_w, incs=[(cCONST, 1)], tag="ynzero")
        for q in range(4):
            P.op("gpsimd", lambda e, q=q: e.dma_start(
                Wfv_q[q][:], bass.AP(Wfv_h.tensor, q * 8 * 128 * D,
                                     [[D, 128], [128 * D, NT], [1, D]])),
                waits=ph1_w, incs=[(cWFVQ[q], 16)], tag=f"wfvq{q}")
        P.op("gpsimd", lambda e: e.dma_start(
            Wfr_sb[:], bass.AP(Wfr_h.tensor, 0,
                               [[D, 128], [128 * D, NT], [1, D]])),
            waits=ph1_w, incs=[(cWFR, 16)], tag="wfr")

        cXNLD = ch("xnld")
        cYM = ch("ymg")
        cBC2 = ch("bcopy2")
        cRRMM = ch("rrmm")
        cRR = ch("rrs")
        cKKMM = ch("kkmm")
        cRELU = ch("relu")
        cKKSQ = ch("kksq")
        cFFJ = ch("ffj")
        cFFD = ch("ffdone")
        cFF1 = ch("ff1v")
        cX2E = ch("x2evac")
        cX2ST = ch("x2st")

        def ynsl(d, o, w):
            return yn1[:, d * (C2 + 1) + o: d * (C2 + 1) + o + w]

        def l2_ln(c):
            P.op("sync", lambda e, c=c: e.dma_start(
                x_dst(xnin, C2), x_src(xnew_hbm, c, C2)),
                waits=[(cXNST, 16 * 8 * NC1), (cX2E, cX2E.n),
                       (cCAST, cCAST.n), (cSQ, cSQ.n), (cAP1, cAP1.n)],
                incs=[(cXNLD, 16)], tag=f"xnld{c}")
            emit_ln_stats(
                C2, lambda d: xsl(xnin, d, C2), yb2, yq2, 6, 0, 7, 0,
                src_waits=[(cXNLD, 16 * (c + 1))], tag=f"l2s{c}_")
            emit_ln_apply(
                C2, lambda d: xsl(xnin, d, C2), gb2, D,
                lambda d: ynsl(d, 1, C2), ytmp2, c % 2, c % 2,
                [6, 7], src_waits=[(cXNLD, 16 * (c + 1))],
                extra_apply_waits=[cYM, cBC2, cRRMM, cX2E],
                repl_waits=[(CHS["gb_ld"], 64)],
                tag=f"l2a{c}_")
            apply_after[("l2", c)] = cAPPLY.n

        JORD = list(range(16, 32)) + list(range(16))

        def l2_chunk(c):
            aw = (cAPPLY, apply_after[("l2", c)])
            for d in range(NT):
                P.op("gpsimd",
                     lambda e, d=d: e.tensor_tensor(
                         xsl(ym1, d, C2), ynsl(d, 1, C2), ynsl(d, 0, C2),
                         ALU.add),
                     waits=[aw, (cKKMM, cKKMM.n), (cCONST, cCONST.n)],
                     incs=[(cYM, 1)],
                     tag=f"ym{c}_{d}")
            for d in range(NT):
                if c + 1 < NC2:
                    P.op("gpsimd",
                         lambda e, d=d: e.tensor_copy(ynsl(d, 0, 1),
                                                      ynsl(d, C2, 1)),
                         waits=[(cYM, 8 * (c + 1))],
                         incs=[(cBC2, 1)], tag=f"bc2{c}_{d}")
            def rrsig(step):
                P.op("scalar", lambda e: e.drain(), tag=f"rrdr{c}_{step}")
                if step < NT:
                    d = step
                    rb_ = 4 + d % 2
                    mm_group(rb_, 0, C2, Wfr_sb, lambda kt: ynsl(kt, 1, C2), d,
                             waits=[aw, (cWFR, 16), bCH[rb_]],
                             incs=[(cRRMM, 1)], tag=f"rrmm{c}_{d}")
                    rr_w = [(cRRMM, cRRMM.n)]
                    if c > 0:
                        rr_w.append((cX2E, 8 * c))
                    P.op("scalar",
                         lambda e, d=d, rb_=rb_: e.activation(
                             rtmp2[d % 3][:, 0:C2], bk[rb_][:, 0:C2],
                             ACT.Exp, scale=-1.0),
                         waits=rr_w, incs=[(bCH[rb_], 1)], tag=f"rr1_{c}_{d}")
                if step >= 1 and step - 1 < NT:
                    d = step - 1
                    P.op("scalar",
                         lambda e, d=d: e.activation(
                             rtmp2[d % 3][:, 0:C2], rtmp2[d % 3][:, 0:C2],
                             ACT.Ln, bias=1.0), tag=f"rr2_{c}_{d}")
                if step >= 2:
                    d = step - 2
                    P.op("scalar",
                         lambda e, d=d: e.activation(
                             xsl(rr1, d, C2), rtmp2[d % 3][:, 0:C2],
                             ACT.Exp, scale=-1.0),
                         incs=[(cRR, 1)], tag=f"rr3_{c}_{d}")

            for step in range(NT + 2):
                rrsig(step)
            for jj, j in enumerate(JORD):
                q = j // 8
                kb = 4 + jj % 2
                mm_group(kb, 0, C2, Wfk_q[q], lambda kt: xsl(ym1, kt, C2),
                         j % 8,
                         waits=[(cYM, 8 * (c + 1)), (cWFKQ[q], 16),
                                bCH[kb], (cRELU, max(0, cRELU.n - 1))],
                         incs=[(cKKMM, 1)], tag=f"kkmm{c}_{j}")
                P.op("scalar",
                     lambda e, jj=jj, kb=kb: e.activation(
                         kk2[jj % 2][:, 0:C2], bk[kb][:, 0:C2], ACT.Relu,
                         scale=0.5),
                     waits=[(cKKMM, cKKMM.n), (cFFJ, max(0, cFFJ.n - 1))],
                     incs=[(cRELU, 1)], tag=f"relu{c}_{j}")
                P.op("vector",
                     lambda e, jj=jj: e.tensor_tensor(
                         kk2[jj % 2][:, 0:C2], kk2[jj % 2][:, 0:C2],
                         kk2[jj % 2][:, 0:C2], ALU.mult),
                     waits=[(cRELU, cRELU.n)], incs=[(cKKSQ, 1)],
                     tag=f"kksq{c}_{j}")

                ff_w = [(cKKSQ, cKKSQ.n), (cWFVQ[q], 16)]
                if jj == 0:
                    ff_w.append((cFF1, 8 * c))
                if jj < 31:
                    def f_ff(e, jj=jj, j=j, q=q):
                        inst = None
                        for i in range(NT):
                            bt = bk[i // 2]
                            off = (i % 2) * C2
                            inst = e.matmul(
                                bt[:, off:off + C2],
                                Wfv_q[q][:, (j % 8) * D + i * 128:
                                         (j % 8) * D + i * 128 + 128],
                                kk2[jj % 2][:, 0:C2],
                                start=(jj == 0 and i % 2 == 0), stop=False)
                        return inst
                    P.op("tensor", f_ff, waits=ff_w, incs=[(cFFJ, 1)],
                         tag=f"ff{c}_{j}")
                else:
                    for i in range(NT):
                        def f_ffc(e, jj=jj, j=j, q=q, i=i):
                            bt = bk[i // 2]
                            off = (i % 2) * C2
                            return e.matmul(
                                bt[:, off:off + C2],
                                Wfv_q[q][:, (j % 8) * D + i * 128:
                                         (j % 8) * D + i * 128 + 128],
                                kk2[jj % 2][:, 0:C2],
                                start=False, stop=(i % 2 == 1))
                        P.op("tensor", f_ffc, waits=(ff_w if i == 0 else []),
                             incs=([(cFFJ, 1)] if i == NT - 1 else []),
                             tag=f"ffc{c}_{i}")
                    P.op("tensor", lambda e: e.drain(), incs=[(cFFD, 1)],
                         tag=f"ffdr{c}")
            def ffe2(i):
                P.op("gpsimd",
                     lambda e, i=i: e.tensor_tensor(
                         x2cr[i % 2][:, 0:C2], ytmp2[i % 3][:, 0:C2],
                         xsl(xnin, i, C2), ALU.add),
                     waits=[(cX2ST, max(0, 16 * (8 * c + i - 1))),
                            (cFF1, 8 * c + i + 1), (cXNLD, 16 * (c + 1))],
                     incs=[(cX2E, 1)], tag=f"ffe2_{c}_{i}")
                P.op("sync",
                     lambda e, c=c, i=i: e.dma_start(
                         bass.AP(x2_hbm.tensor, i * 128 * L + c * C2,
                                 [[L, 128], [1, C2]]),
                         x2cr[i % 2][:, 0:C2]),
                     waits=[(cX2E, cX2E.n)], incs=[(cX2ST, 16)],
                     tag=f"x2st{c}_{i}")

            for i in range(NT + 1):
                if i < NT:
                    bt = bk[i // 2]
                    off = (i % 2) * C2
                    P.op("vector",
                         lambda e, i=i, bt=bt, off=off: e.tensor_tensor(
                             ytmp2[i % 3][:, 0:C2], bt[:, off:off + C2],
                             xsl(rr1, i, C2), ALU.mult),
                         waits=[(cFFD, c + 1), (cRR, 8 * (c + 1)),
                                (cX2E, max(0, 8 * c + i - 1))],
                         incs=[(cFF1, 1)],
                         tag=f"ffe1_{c}_{i}")
                if i >= 1:
                    ffe2(i - 1)
            if c + 1 < NC2:
                l2_ln(c + 1)

    if phases >= 2:
        l2_ln(0)
        for c in range(NC2):
            l2_chunk(c)
    if phases < 2:
        pass

    if phases < 3:
        P.op("sync", lambda e: e.nop(),
             waits=[(cX2ST, 16 * 8 * NC2)] if phases == 2 else [],
             tag="final_p2")
    if phases >= 4:
        ph2_w = [(chn, chn.n) for chn in
                 [cX2ST, cX2E, cFF1, cFFD, cFFJ, cRR, cAPPLY, cAP1, cYM,
                  cBC2, cSQ, cSTC, cREPL, cXNLD, cRELU, cKKSQ, cKKMM, cRRMM]]
        cEXPW = [ch(f"expw{i}") for i in range(E)]
        for ei in range(2):
            P.op("gpsimd", lambda e, ei=ei: e.dma_start(
                WexpR[ei][:], bass.AP(WexpT_h.tensor, ei * D * D,
                                      [[D, 128], [128 * D, NT], [1, D]])),
                waits=ph2_w, incs=[(cEXPW[ei], 16)], tag=f"expw{ei}")

        cX2LD = ch("x2ld")
        cXST3 = ch("xst3")
        cXAD3 = ch("xad3")
        cX3B = ch("x3b")
        cRTMM = ch("rtmm")
        cRTCP = ch("rtcp")
        cNM = ch("negmax")
        cEEXP = ch("eexp")
        cRW = ch("rwv")
        cTRS = ch("trsp")
        cRWT = ch("rwt")

        def accsl(d, o, w):
            return ACC[:, d * L + o: d * L + o + w]

        def x3bsl(d, o, w):
            return X3b[:, d * L + o: d * L + o + w]

        for c in range(NC3):
            P.op("sync", lambda e, c=c: e.dma_start(
                bass.AP(x2in_oc, 0, [[NT * C3, 128], [1, NT * C3]]),
                bass.AP(x2_hbm.tensor, c * C3,
                        [[L, 128], [128 * L, NT], [1, C3]])),
                waits=ph2_w + [(cX2ST, 16 * 8 * (2 * c + 2)),
                               (cCAST, cCAST.n), (cSQ, cSQ.n),
                               (cAP1, cAP1.n)],
                incs=[(cX2LD, 16)], tag=f"x2ld{c}")
            P.op("sync", lambda e, c=c: e.dma_start(
                bass.AP(xstg, 0, [[NT * C3, 128], [1, NT * C3]]),
                bass.AP(xT.tensor, c * C3,
                        [[L, 128], [128 * L, NT], [1, C3]])),
                waits=ph2_w + ([(cXAD3, c)] if c else []),
                incs=[(cXST3, 16)], tag=f"xst3_{c}")
            for hh in range(2):
                P.op("gpsimd", lambda e, hh=hh: e.tensor_tensor(
                    x2in_oc[:, hh * 2048: hh * 2048 + 2048],
                    x2in_oc[:, hh * 2048: hh * 2048 + 2048],
                    xstg[:, hh * 2048: hh * 2048 + 2048], ALU.add),
                    waits=[(cX2LD, 16 * (c + 1)), (cXST3, 16 * (c + 1))],
                    incs=([(cXAD3, 1)] if hh else []), tag=f"xadd{c}_{hh}")
            emit_ln_stats(
                C3, lambda d: xsl(x2in_oc, d, C3), cb2, cq2, 6, 0, 5, 0,
                src_waits=[(cXAD3, c + 1)],
                extra_stats_waits=([(cRELU, cRELU.n)] if c == 0 else []),
                tag=f"pl1s{c}_")
            emit_ln_apply(
                C3, lambda d: xsl(x2in_oc, d, C3), gbu, 128,
                lambda d, c=c: accsl(d, c * C3, C3), ctmp2, c % 2, c % 2,
                [6, 7], src_waits=[(cXAD3, c + 1)],
                repl_waits=[(CHS["gbu_ld"], 32)],
                tag=f"pl1a{c}_")
            apply_after[("l3", c)] = cAPPLY.n
            aw = (cAPPLY, cAPPLY.n)
            for d in range(NT):
                P.op("scalar",
                     lambda e, d=d, c=c: e.copy(x3bsl(d, c * C3, C3),
                                                accsl(d, c * C3, C3)),
                     waits=[aw], incs=[(cX3B, 1)], tag=f"x3b{c}_{d}")
            for m in range(C3 // 128) if phases >= 5 else []:
                def f_rt(e, c=c, m=m):
                    inst = None
                    for kt in range(NT):
                        inst = e.matmul(
                            bk[4][:, (4 * c + m) * 8: (4 * c + m) * 8 + 8],
                            accsl(kt, c * C3 + m * 128, 128),
                            WrT_sb[:, kt * E: kt * E + E],
                            start=(kt == 0), stop=False)
                    return e.matmul(
                        bk[4][:, (4 * c + m) * 8: (4 * c + m) * 8 + 8],
                        onesrow_f32[0:1, :], brout_row[0:1, :],
                        start=False, stop=True)
                w_rt = [aw, (cVEC, 7 * 16)]
                if c == 0 and m == 0:
                    w_rt.append((cRELU, cRELU.n))
                P.op("tensor", f_rt, waits=w_rt, incs=[(cRTMM, 1)],
                     tag=f"rtmm{c}_{m}")
            if phases >= 5:
                P.op("vector",
                     lambda e, c=c: e.tensor_copy(
                         rw_sb[:, 32 * c: 32 * c + 32],
                         bk[4][:, 32 * c: 32 * c + 32]),
                     waits=[(cRTMM, cRTMM.n)],
                     incs=[(cRTCP, 1)],
                     tag=f"rtcp{c}")

        P.op("gpsimd", lambda e: e.dma_start(
            WexpR[2][:], bass.AP(WexpT_h.tensor, 2 * D * D,
                                 [[D, 128], [128 * D, NT], [1, D]])),
            waits=[(cXAD3, NC3)], incs=[(cEXPW[2], 16)], tag="expw2")
        if phases == 4:
            P.op("sync", lambda e: e.nop(), waits=[(cX3B, 8 * NC3)],
                 tag="final_p4")
        SOFT = phases >= 5
        def rsl(m):
            return rw_sb[:, 8 * m: 8 * m + 8]

        P.op("vector", lambda e: e.drain(), tag="rtdr") if SOFT else None
        for g in range(4) if SOFT else []:
            ms = list(range(4 * g, 4 * g + 4))
            for m in ms:
                P.op("vector", lambda e, m=m: e.reduce_max(
                    rt[:, m % 4: m % 4 + 1], rsl(m), axis=AX.X, negate=True),
                    waits=[(cRTCP, NC3), (cEEXP, cEEXP.n)],
                    incs=[(cNM, 1)], tag=f"nm{m}")
            P.op("vector", lambda e: e.drain(), tag=f"rdr{g}a")
            for m in ms:
                P.op("scalar", lambda e, m=m: e.activation(
                    rsl(m), rsl(m), ACT.Exp, bias=rt[:, m % 4: m % 4 + 1]),
                    waits=[(cNM, cNM.n)], incs=[(cEEXP, 1)], tag=f"ee{m}")
            for m in ms:
                P.op("vector", lambda e, m=m: e.reduce_sum(
                    rt[:, 4 + m % 4: 5 + m % 4], rsl(m), axis=AX.X),
                    waits=[(cEEXP, cEEXP.n)], tag=f"ss{m}")
            P.op("vector", lambda e: e.drain(), tag=f"rdr{g}b")
            for m in ms:
                P.op("vector", lambda e, m=m: e.reciprocal(
                    rt[:, 8 + m % 4: 9 + m % 4], rt[:, 4 + m % 4: 5 + m % 4]),
                    tag=f"rc{m}")
            P.op("vector", lambda e: e.drain(), tag=f"rdr{g}c")
            for m in ms:
                P.op("vector", lambda e, m=m: e.tensor_scalar_mul(
                    rsl(m), rsl(m), rt[:, 8 + m % 4: 9 + m % 4]),
                    tag=f"pr{m}")
            P.op("vector", lambda e: e.drain(), tag=f"rdr{g}d")
            for m in ms:
                P.op("vector", lambda e, m=m: e.reduce_max(
                    rt[:, 12 + m % 4: 13 + m % 4], rsl(m), axis=AX.X),
                    tag=f"m1_{m}")
            P.op("vector", lambda e: e.drain(), tag=f"rdr{g}e")
            for m in ms:
                P.op("vector", lambda e, m=m: e.tensor_single_scalar(
                    rt[:, 24 + (m % 4) * 8: 32 + (m % 4) * 8], rsl(m),
                    rt[:, 12 + m % 4: 13 + m % 4], ALU.is_ge),
                    tag=f"mk_{m}")
            P.op("vector", lambda e: e.drain(), tag=f"rdr{g}h")
            for m in ms:
                P.op("vector", lambda e, m=m: e.scalar_tensor_tensor(
                    rt[:, 56 + (m % 4) * 8: 64 + (m % 4) * 8],
                    rt[:, 24 + (m % 4) * 8: 32 + (m % 4) * 8], -1e30,
                    rsl(m), ALU.mult, ALU.add),
                    tag=f"pm_{m}")
            P.op("vector", lambda e: e.drain(), tag=f"rdr{g}i")
            for m in ms:
                P.op("vector", lambda e, m=m: e.reduce_max(
                    rt[:, 16 + m % 4: 17 + m % 4],
                    rt[:, 56 + (m % 4) * 8: 64 + (m % 4) * 8], axis=AX.X),
                    tag=f"m2_{m}")
            P.op("vector", lambda e: e.drain(), tag=f"rdr{g}f")
            for m in ms:
                P.op("vector", lambda e, m=m: e.tensor_single_scalar(
                    rt[:, 24 + (m % 4) * 8: 32 + (m % 4) * 8], rsl(m),
                    rt[:, 16 + m % 4: 17 + m % 4], ALU.is_ge),
                    tag=f"kp_{m}")
            P.op("vector", lambda e: e.drain(), tag=f"rdr{g}g")
            for m in ms:
                P.op("vector", lambda e, m=m: e.tensor_tensor(
                    rsl(m), rsl(m),
                    rt[:, 24 + (m % 4) * 8: 32 + (m % 4) * 8], ALU.mult),
                    incs=[(cRW, 1)], tag=f"rw_{m}")

        if phases == 5:
            P.op("sync", lambda e: e.nop(), waits=[(cRW, 16)],
                 tag="final_p5")
        for m in range(16) if SOFT else []:
            P.op("tensor",
                 lambda e, m=m: e.transpose(
                     bk[4 + m % 4][0:8, 0:128],
                     rsl(m), ident[:]),
                 waits=[(cRW, m + 1), (cIDENT, 1), (cRTCP, NC3),
                        (cRWT, max(0, cRWT.n - 3))],
                 incs=[(cTRS, 1)], tag=f"trs{m}")
            P.op("vector",
                 lambda e, m=m: e.tensor_copy(
                     rwT_sb[0:8, 128 * m: 128 * m + 128],
                     bk[4 + m % 4][0:8, 0:128]),
                 waits=[(cTRS, m + 1)], incs=[(cRWT, 1)],
                 tag=f"rwte{m}")

        if dbg and cX3B.n > 0:
            P.op("sync", lambda e: e.dma_start(dbg_x3, ACC[:, :]),
                 waits=[(cX3B, 8 * NC3)], incs=[(cDBG, 16)], tag="dbgx3")
        if dbg and SOFT:
            P.op("sync", lambda e: e.dma_start(dbg_rw, rw_sb[:]),
                 waits=[(cRW, 16)], incs=[(cDBG, 16)], tag="dbgrw")

        if phases == 6:
            P.op("sync", lambda e: e.nop(), waits=[(cRWT, 16)],
                 tag="final_p6pre")
        cRWB = ch("rwb")
        cZ = ch("zv")
        cZF = ch("zfree")
        cPMM = ch("pmm")
        cPEV = ch("pev")

        for p in range(4) if phases >= 6 else []:
            for c3 in range(NCM):
                for ei, ex in enumerate((2 * p, 2 * p + 1)):
                    rwb_w = [(cRWT, 16), (cZ, max(0, cZ.n - 1))]
                    if dbg and p == 0 and c3 == 0:
                        rwb_w.append((cDBG, cDBG.n))
                    P.op("sync",
                         lambda e, ex=ex, ei=ei, c3=c3: e.dma_start(
                             rwrep2[ei][:, 0:CM],
                             bass.AP(rwT_sb, ex * L + c3 * CM,
                                     [[L, 1], [0, 128], [1, CM]])),
                         waits=rwb_w, incs=[(cRWB, 16)],
                         tag=f"rwb{p}_{c3}_{ei}")
                    P.op("vector",
                         lambda e, ei=ei, c3=c3: e.tensor_tensor(
                             bass.AP(Zb[ei], 0,
                                     [[NT * CM, 128], [CM, NT], [1, CM]]),
                             bass.AP(X3b, c3 * CM,
                                     [[NT * L, 128], [L, NT], [1, CM]]),
                             bass.AP(rwrep2[ei], 0,
                                     [[CM, 128], [0, NT], [1, CM]]),
                             ALU.mult),
                         waits=[(cRWB, cRWB.n), (cX3B, 8 * NC3),
                                (cPMM, max(0, NCM * p + c3 - 1))],
                         incs=[(cZ, 1)], tag=f"z{p}_{c3}_{ei}")

                    pm_w = [(cZ, cZ.n), (cEXPW[ex], 16)]
                    if ei == 0:
                        if p == 0 and c3 == 0:
                            pm_w.append((cFF1, cFF1.n))
                        elif not (p == 0 and c3 == 0):
                            pm_w.append((cPEV, 8 * (NCM * p + c3)))
                    if ei == 0 or p == 0:
                        def f_pm(e, ex=ex, ei=ei):
                            inst = None
                            wbuf = WexpR[ex % 3]
                            for i in range(NT):
                                for kt in range(NT):
                                    inst = e.matmul(
                                        bk[i][:, 0:CM],
                                        wbuf[:, kt * D + i * 128:
                                             kt * D + i * 128 + 128],
                                        Zb[ei][:, kt * CM: kt * CM + CM],
                                        start=(ei == 0 and kt == 0),
                                        stop=False)
                            return inst
                        P.op("tensor", f_pm, waits=pm_w,
                             tag=f"pmm{p}_{c3}_{ei}")
                    else:
                        for i in range(NT):
                            def f_pmc(e, ex=ex, ei=ei, i=i):
                                inst = None
                                wbuf = WexpR[ex % 3]
                                for kt in range(NT):
                                    inst = e.matmul(
                                        bk[i][:, 0:CM],
                                        wbuf[:, kt * D + i * 128:
                                             kt * D + i * 128 + 128],
                                        Zb[ei][:, kt * CM: kt * CM + CM],
                                        start=False,
                                        stop=(kt == NT - 1))
                                return inst
                            P.op("tensor", f_pmc,
                                 waits=(pm_w if i == 0 else []),
                                 tag=f"pmmc{p}_{c3}_{i}")
                if p == 0:
                    for i in range(NT):
                        def f_bias(e, c3=c3, i=i):
                            return e.matmul(
                                bk[i][:, 0:CM],
                                bexp_sb[0:E, i * 128: i * 128 + 128],
                                bass.AP(rwT_sb, c3 * CM, [[L, E], [1, CM]]),
                                start=False, stop=True)
                        P.op("tensor", f_bias,
                             waits=([(cRWT, 16), (cGB, cGB.n)] if i == 0
                                    else []),
                             tag=f"pbias{c3}_{i}")
                P.op("tensor", lambda e: e.drain(), incs=[(cPMM, 1)],
                     tag=f"pmdr{p}_{c3}")
                for i in range(NT):
                    P.op("vector",
                         lambda e, i=i, c3=c3: e.tensor_tensor(
                             accsl(i, c3 * CM, CM), bk[i][:, 0:CM],
                             accsl(i, c3 * CM, CM), ALU.add),
                         waits=[(cPMM, NCM * p + c3 + 1)],
                         incs=[(cPEV, 1)],
                         tag=f"pev{p}_{c3}_{i}")
            for nx in (2 * p + 3, 2 * p + 4):
                if nx < E:
                    prev = nx - 3
                    pm_val = NCM * (prev // 2 + 1)
                    P.op("gpsimd", lambda e, nx=nx: e.dma_start(
                        WexpR[nx % 3][:],
                        bass.AP(WexpT_h.tensor, nx * D * D,
                                [[D, 128], [128 * D, NT], [1, D]])),
                        waits=[(cPMM, pm_val)],
                        incs=[(cEXPW[nx], 16)], tag=f"expw{nx}")

        if phases == 6:
            P.op("sync", lambda e: e.nop(),
                 waits=[(cPEV, 8 * NCM * 4)], tag="final_p6")
        cOUT = ch("outst")
        for c in range(NC3) if phases >= 9 else []:
            emit_ln_stats(
                C3, lambda d, c=c: accsl(d, c * C3, C3), cb2, cq2, 6, 0, 5, 0,
                src_waits=[(cPEV, cPEV.n)], tag=f"pl2s{c}_")
            emit_ln_apply(
                C3, lambda d, c=c: accsl(d, c * C3, C3), gbu, 128,
                lambda d: xsl(x2in_oc, d, C3), ctmp2, c % 2, c % 2,
                [6, 7], src_waits=[(cPEV, cPEV.n)],
                extra_apply_waits=[(cOUT, 16 * c)],
                repl_waits=[(CHS["gbu_ld"], 32)],
                tag=f"pl2a{c}_")
            P.op("sync",
                 lambda e, c=c: e.dma_start(
                     bass.AP(outT.tensor, c * C3,
                             [[L, 128], [128 * L, NT], [1, C3]]),
                     bass.AP(x2in_oc, 0, [[NT * C3, 128], [1, NT * C3]])),
                 waits=[(cAPPLY, cAPPLY.n)], incs=[(cOUT, 16)],
                 tag=f"out{c}")
        if phases >= 9:
            P.op("sync", lambda e: e.nop(), waits=[(cOUT, 16 * NC3)],
                 tag="final")

    global _LAST_PROG
    _LAST_PROG = P

    with nc.Block() as blk:
        @blk.sync
        def _(eng):
            P.replay("sync", eng)

        @blk.gpsimd
        def _(eng):
            P.replay("gpsimd", eng)

        @blk.tensor
        def _(eng):
            P.replay("tensor", eng)

        @blk.vector
        def _(eng):
            P.replay("vector", eng)

        @blk.scalar
        def _(eng):
            P.replay("scalar", eng)

    return nc


def kernel(**inputs):
    x = np.asarray(inputs["L0_attn_x"])
    n = 8
    common = {
        "Wk": np.asarray(inputs["L0_attn_Wk"]),
        "Wv": np.asarray(inputs["L0_attn_Wv"]),
        "Wr": np.asarray(inputs["L0_attn_Wr"]),
        "Wo": np.asarray(inputs["L0_attn_Wo"]),
        "Wfk": np.asarray(inputs["L0_attn_Wfk"]),
        "Wfv": np.asarray(inputs["L0_attn_Wfv"]),
        "Wfr": np.asarray(inputs["L0_attn_Wfr"]),
        "WexpT": np.ascontiguousarray(
            np.asarray(inputs["L0_ffn_W_experts"]).transpose(0, 2, 1)),
        "WrouterT": np.ascontiguousarray(
            np.asarray(inputs["L0_ffn_W_router"]).T),
        "ln1_g": np.asarray(inputs["L0_attn_ln1_g"]),
        "ln1_b": np.asarray(inputs["L0_attn_ln1_b"]),
        "ln2_g": np.asarray(inputs["L0_attn_ln2_g"]),
        "ln2_b": np.asarray(inputs["L0_attn_ln2_b"]),
        "mix_k": np.asarray(inputs["L0_attn_mix_k"]),
        "mix_v": np.asarray(inputs["L0_attn_mix_v"]),
        "mix_r": np.asarray(inputs["L0_attn_mix_r"]),
        "td": np.asarray(inputs["L0_attn_time_decay"]),
        "tf": np.asarray(inputs["L0_attn_time_first"]),
        "b_experts": np.asarray(inputs["L0_ffn_b_experts"]),
        "b_router": np.asarray(inputs["L0_ffn_b_router"]),
    }
    in_maps = []
    for b in range(n):
        m = dict(common)
        m["xT"] = np.ascontiguousarray(x[b].T)
        in_maps.append(m)
    nc = build_nc()
    res = run_bass_kernel_spmd(nc, in_maps, core_ids=list(range(n)))
    out = np.stack([np.asarray(res.results[b]["outT"]).T for b in range(n)])
    return out.astype(np.float32)


def build_prog_for_check(dbg=False, phases=9):
    build_nc(dbg=dbg, phases=phases)
    return _LAST_PROG, None
